# version 02
# baseline (speedup 1.0000x reference)
import os
import numpy as np
import ml_dtypes
import concourse.bass as bass
import concourse.mybir as mybir
from concourse.bass_utils import run_bass_kernel_spmd

F32 = mybir.dt.float32
BF16 = mybir.dt.bfloat16
I32 = mybir.dt.int32
AF = mybir.ActivationFunctionType
ALU = mybir.AluOpType
AX = mybir.AxisListType

D = 1024
H = 16
NDC = D // 128
TA = 256
TB = 512
W7 = 7
KW = 134


class Sched:
    ENGS = ("sync", "tensor", "vector", "scalar", "gpsimd")

    def __init__(self, nc):
        self.nc = nc
        self.prog = {e: [] for e in self.ENGS}
        self.sems = {}
        self.cnt = {}

    def sem(self, name):
        self.sems[name] = self.nc.alloc_semaphore(name)
        self.cnt[name] = 0

    def op(self, eng, fn, inc=None, dma=False):
        incs = []
        ret = None
        if inc is not None:
            k = 16 if dma else 1
            self.cnt[inc] += k
            ret = self.cnt[inc]
            incs.append((self.sems[inc], k))

        def run(e, fn=fn, incs=incs):
            inst = fn(e)
            for h, k in incs:
                inst.then_inc(h, k)
        self.prog[eng].append(run)
        return ret

    def wait(self, eng, sem, val):
        if val is None or val <= 0:
            return
        h = self.sems[sem]
        self.prog[eng].append(lambda e, h=h, v=val: e.wait_ge(h, v))

    def barrier(self):
        snap = {s: self.cnt[s] for s in ("t", "v", "s", "g", "ld", "st")}
        own = {"tensor": "t", "vector": "v", "scalar": "s", "gpsimd": "g"}
        for eng in self.ENGS:
            for s, v in snap.items():
                if own.get(eng) != s:
                    self.wait(eng, s, v)


def build_program(S):
    assert S % TB == 0
    NT_A = S // TA
    NT_B = S // TB
    CH_A = TA // 128
    CH_B = TB // 128
    NCHUNK = S // 128

    nc = bass.Bass("TRN2", target_bir_lowering=False)

    xio = nc.dram_tensor("x", [S, D], F32, kind="ExternalInput").ap()
    wqT = nc.dram_tensor("wqT", [D, D], F32, kind="ExternalInput").ap()
    wkT = nc.dram_tensor("wkT", [D, D], F32, kind="ExternalInput").ap()
    wvT = nc.dram_tensor("wvT", [D, D], F32, kind="ExternalInput").ap()
    woT = nc.dram_tensor("woT", [D, D], F32, kind="ExternalInput").ap()
    wi8T = nc.dram_tensor("wi8T", [D, D], I32, kind="ExternalInput").ap()
    bo_rep = nc.dram_tensor("bo_rep", [128, D], F32, kind="ExternalInput").ap()
    bf_rep = nc.dram_tensor("bf_rep", [128, D], F32, kind="ExternalInput").ap()
    ws_in = nc.dram_tensor("ws", [1, 1], F32, kind="ExternalInput").ap()
    pos_c = nc.dram_tensor("pos_t", [H, 224], F32, kind="ExternalInput").ap()
    cw1_c = nc.dram_tensor("cw1r", [H, 224], F32, kind="ExternalInput").ap()
    cb1_c = nc.dram_tensor("cb1r", [H, 224], F32, kind="ExternalInput").ap()
    cw2_c = nc.dram_tensor("cw2r", [H, 1024], F32, kind="ExternalInput").ap()
    cb2_c = nc.dram_tensor("cb2r", [H, 224], F32, kind="ExternalInput").ap()
    cw3_c = nc.dram_tensor("cw3r", [H, 224], F32, kind="ExternalInput").ap()
    cb3_c = nc.dram_tensor("cb3c", [H, 1], F32, kind="ExternalInput").ap()
    ident_c = nc.dram_tensor("ident", [128, 128], F32, kind="ExternalInput").ap()
    ones1_c = nc.dram_tensor("ones1", [1, 128], F32, kind="ExternalInput").ap()
    m1_c = nc.dram_tensor("m1", [128, KW], F32, kind="ExternalInput").ap()
    dmask_c = nc.dram_tensor("dmask", [128, W7 * KW], F32, kind="ExternalInput").ap()

    out_d = nc.dram_tensor("out", [S, D], F32, kind="ExternalOutput").ap()
    _dbga = os.environ.get("K_DEBUG_ATTN") == "1"
    if _dbga:
        NT_A_ = S // TA
        dbg_q = nc.dram_tensor("dbg_q", [NT_A_, 128, NDC * TA], F32, kind="ExternalOutput").ap()
        dbg_k = nc.dram_tensor("dbg_k", [NT_A_, 128, NDC * (TA + 6)], F32, kind="ExternalOutput").ap()
        dbg_v = nc.dram_tensor("dbg_v", [NT_A_, 128, CH_A * D], F32, kind="ExternalOutput").ap()
        dbg_vh = nc.dram_tensor("dbg_vh", [NT_A_, 128, D], F32, kind="ExternalOutput").ap()
        dbg_k2 = nc.dram_tensor("dbg_k2", [NT_A_, 128, NDC * (TA + 6)], F32, kind="ExternalOutput").ap()
        dbg_sall = nc.dram_tensor("dbg_sall", [S // 128, 128, H * KW], F32, kind="ExternalOutput").ap()
    _dbg = os.environ.get("K_DEBUG_SCRATCH") == "1"
    _scr = "ExternalOutput" if _dbg else "Internal"
    attn_d = nc.dram_tensor("attn_scr", [128, NDC, S], F32, kind=_scr).ap()
    xint_d = nc.dram_tensor("xint_scr", [128, NDC, S], BF16, kind=_scr).ap()
    conn_d = nc.dram_tensor("conn_scr", [H, W7], F32, kind=_scr).ap()

    sbt = lambda name, shape, dt=F32: nc.alloc_sbuf_tensor(name, shape, dt)

    wn = {}
    for n in ("wq", "wk", "wv"):
        wn[n + "_hi"] = sbt(f"{n}_hi", [128, NDC * D], BF16)
        wn[n + "_lo"] = sbt(f"{n}_lo", [128, NDC * D], BF16)
    wi8_sb = sbt("wi8", [128, NDC * D], BF16)

    offs = {}
    cur = [0]

    def asl_mark(name, n, align=8):
        o = (cur[0] + align - 1) // align * align
        offs[name] = (o, n)
        cur[0] = o + n
        return o

    for nm, n in [("xs0", D), ("xs1", D), ("xn_tm", D),
                  ("qbuf", NDC * TA), ("kb0", NDC * (TA + 6)), ("kb1", NDC * (TA + 6)),
                  ("vbuf", CH_A * D), ("vhead", D), ("sall", H * KW),
                  ("qta", 512), ("qtb", 512), ("attn_b", 512),
                  ("u_b", 512), ("c_b", 512), ("xi_tm", D), ("cbt", H * KW),
                  ("m1t", KW), ("ident", 128)]:
        asl_mark(nm, n)
    A_END = cur[0]
    B_layout = [("ab", CH_B * D), ("xb2", CH_B * D), ("g_tm", CH_B * D),
                ("bo_t", D), ("bf_t", D)]
    bo = 0
    boffs = {}
    for nm, n in B_layout:
        boffs[nm] = (bo, n)
        bo += n
    AR_F32 = max(A_END, bo) + 16
    ar = sbt("arena_f32", [128, AR_F32], F32)

    def A(nm):
        o, n = offs[nm]
        return ar[:, o:o + n]

    def Bs(nm):
        o, n = boffs[nm]
        return ar[:, o:o + n]

    xslot = [A("xs0"), A("xs1")]
    xn_tm = A("xn_tm")
    qbuf = A("qbuf")
    kbuf = [A("kb0"), A("kb1")]
    vbuf = A("vbuf")
    vhead = A("vhead")
    sall = A("sall")
    qta = A("qta")
    qtb = A("qtb")
    attn_b = A("attn_b")
    u_b = A("u_b")
    c_b = A("c_b")
    xi_tm = A("xi_tm")
    cbt = A("cbt")
    m1t = A("m1t")
    ident = A("ident")
    dmk = ar[:, offs["qbuf"][0]:offs["qbuf"][0] + W7 * KW]
    stage = ar[:, offs["kb1"][0]:offs["kb1"][0] + 2 * D]
    cp_base = offs["kb0"][0]
    vb_base = offs["vbuf"][0]

    ab_buf = Bs("ab")
    xb2 = Bs("xb2")
    g_tm = Bs("g_tm")
    bo_t = Bs("bo_t")
    bf_t = Bs("bf_t")

    arb = sbt("arena_b16", [128, 2 * NDC * (TA + 6) + D + NDC * 128 + 32], BF16)
    ob = 0
    xnf_hi = arb[:, ob:ob + NDC * (TA + 6)]; ob += NDC * (TA + 6)
    xnf_lo = arb[:, ob:ob + NDC * (TA + 6)]; ob += NDC * (TA + 6)
    xi_fm = arb[:, ob:ob + NDC * 128]; ob += NDC * 128

    ah_b = wn["wk_hi"][:, 0:NDC * TB]
    al_b = wn["wk_lo"][:, 0:NDC * TB]
    xib = [wn["wv_hi"][:, 0:NDC * TB], wn["wv_lo"][:, 0:NDC * TB]]

    padA = sbt("padA", [128, 8], F32)
    padB = sbt("padB", [128, 8], F32)
    stA = sbt("stA", [128, 16], F32)
    stB = sbt("stB", [128, 16], F32)
    stC = sbt("stC", [128, 16], F32)
    amax = sbt("amax", [128, CH_A], F32)
    sqs = sbt("sqs", [128, NCHUNK], F32)
    stot = sbt("stot", [128, CH_A], F32)
    negm = sbt("negm", [128, 1], F32)
    abd_a = sbt("abd_a", [128, H], F32)
    abd_b = sbt("abd_b", [128, H], F32)
    abd_d = sbt("abd_d", [128, H], F32)
    abd_i = sbt("abd_i", [128, H], F32)
    gs_b = sbt("gs_b", [128, NCHUNK], F32)
    rq_b = sbt("rq_b", [128, NCHUNK], F32)
    ws_b = sbt("ws_b", [128, 1], F32)
    strip_hi = sbt("strip_hi", [128, NDC * 3], BF16)
    strip_lo = sbt("strip_lo", [128, NDC * 3], BF16)
    ident_b = sbt("ident_b", [128, 128], BF16)
    ones1 = sbt("ones1t", [1, 128], F32)
    cfl = sbt("cflat", [1, H * W7], F32)

    ps_tr = [nc.alloc_psum_tensor(f"ps_tr{i}", [128, 512], F32) for i in range(2)]
    ps_pj = [nc.alloc_psum_tensor(f"ps_pj{i}", [128, 512], F32) for i in range(2)]
    ps_sc = [nc.alloc_psum_tensor(f"ps_sc{i}", [128, 3 * KW], F32) for i in range(2)]
    ps_ou = [nc.alloc_psum_tensor(f"ps_ou{i}", [128, 512], F32) for i in range(2)]

    sch = Sched(nc)
    for s in ("ld", "st", "t", "v", "s", "g"):
        sch.sem(s)

    bank_ev = {}
    bank_mm = {}

    def key(ps):
        return id(ps)

    def mm_guard(ps):
        ev = bank_ev.get(key(ps))
        if ev:
            sch.wait("tensor", ev[0], ev[1])

    def mm_done(ps, tval):
        bank_mm[key(ps)] = tval

    def ev_guard(eng, ps):
        sch.wait(eng, "t", bank_mm.get(key(ps), 0))

    def ev_done(ps, sem, val):
        bank_ev[key(ps)] = (sem, val)

    def vpad():
        sch.op("vector", lambda e: e.tensor_copy(padB[:], padA[:]))

    for dst, src in [(ident, ident_c), (ones1[:], ones1_c), (m1t, m1_c),
                     (dmk, dmask_c)]:
        sch.op("sync", lambda e, d=dst, s=src: e.dma_start(out=d, in_=s),
               inc="ld", dma=True)
    cp = {}
    cpo = 0
    for nm, src, n in [("pos", pos_c, 224), ("cw1", cw1_c, 224), ("cb1", cb1_c, 224),
                       ("cw2", cw2_c, 1024), ("cb2", cb2_c, 224)]:
        cp[nm] = ar[0:H, cp_base + cpo:cp_base + cpo + n]
        sch.op("sync", lambda e, d=cp[nm], s=src: e.dma_start(out=d, in_=s),
               inc="ld", dma=True)
        cpo += n
    cpo = 0
    for nm, src, n in [("cw3", cw3_c, 224), ("cb3", cb3_c, 1)]:
        cp[nm] = ar[0:H, vb_base + cpo:vb_base + cpo + n]
        sch.op("sync", lambda e, d=cp[nm], s=src: e.dma_start(out=d, in_=s),
               inc="ld", dma=True)
        cpo += n
    cp["wsv"] = ar[0:1, vb_base + 256:vb_base + 257]
    sch.op("sync", lambda e: e.dma_start(out=cp["wsv"], in_=ws_in), inc="ld", dma=True)
    ld_consts = sch.cnt["ld"]

    last_split = 0
    for wname, wsrc in (("wq", wqT), ("wk", wkT), ("wv", wvT)):
        for piece in range(4):
            dc0 = piece * 2
            sch.wait("sync", "v", last_split)
            lv = sch.op("sync",
                        lambda e, w=wsrc, dc0=dc0: e.dma_start(
                            out=stage,
                            in_=w.rearrange("(c p) o -> p c o", p=128)[:, dc0:dc0 + 2, :]),
                        inc="ld", dma=True)
            sch.wait("vector", "ld", lv)
            for j in range(2):
                dc = dc0 + j
                src = stage[:, j * D:(j + 1) * D]
                hi = wn[wname + "_hi"][:, dc * D:(dc + 1) * D]
                lo = wn[wname + "_lo"][:, dc * D:(dc + 1) * D]
                sch.op("vector", lambda e, s=src, h=hi: e.tensor_copy(h, s))
                last_split = sch.op("vector",
                                    lambda e, s=src, h=hi, l=lo: e.tensor_tensor(l, s, h, ALU.subtract),
                                    inc="v")
    for piece in range(4):
        dc0 = piece * 2
        sch.wait("gpsimd", "v", last_split)
        lv = sch.op("gpsimd",
                    lambda e, dc0=dc0: e.dma_start(
                        out=stage,
                        in_=wi8T.rearrange("(c p) o -> p c o", p=128)[:, dc0:dc0 + 2, :]),
                    inc="ld", dma=True)
        sch.wait("vector", "ld", lv)
        for j in range(2):
            dc = dc0 + j
            src = stage[:, j * D:(j + 1) * D]
            dst = wi8_sb[:, dc * D:(dc + 1) * D]
            last_split = sch.op("vector", lambda e, s=src, d=dst: e.tensor_copy(d, s),
                                inc="v")

    sch.wait("vector", "ld", ld_consts)
    sch.op("vector", lambda e: e.memset(padA[:], 0.0))
    sch.op("vector", lambda e: e.tensor_copy(ident_b[:], ident))
    sch.op("vector", lambda e: e.memset(qtb, 0.0))
    sch.op("vector", lambda e: e.memset(strip_hi[:], 0.0))
    sch.op("vector", lambda e: e.memset(strip_lo[:], 0.0))
    init_v = sch.op("vector", lambda e: e.memset(cbt, 0.0), inc="v")

    h1 = sall[0:H, 0:224]
    h2 = sall[0:H, 256:480]
    tm3 = sall[0:H, 512:736]
    cwt = sall[0:H, 768:775]
    csml = sall[0:H, 800:808]

    sch.op("vector", lambda e: e.tensor_tensor(h1, cp["pos"], cp["cw1"], ALU.mult))
    v1 = sch.op("vector", lambda e: e.tensor_tensor(h1, h1, cp["cb1"], ALU.add), inc="v")
    sch.wait("scalar", "v", v1)
    s1 = sch.op("scalar", lambda e: e.activation(h1, h1, AF.Gelu), inc="s")
    sch.wait("vector", "s", s1)
    sch.op("vector", lambda e: e.memset(h2, 0.0))
    for i in range(32):
        h1_i = h1.rearrange("h (w i) -> h w i", i=32)[:, :, i:i + 1].broadcast_to([H, W7, 32])
        cw2_b = cp["cw2"].rearrange("h (j i) -> h i j", i=32)[:, i:i + 1, :].broadcast_to([H, W7, 32])
        t3 = tm3.rearrange("h (w j) -> h w j", j=32)
        h2_3 = h2.rearrange("h (w j) -> h w j", j=32)
        sch.op("vector", lambda e, a=h1_i, b=cw2_b, d=t3: e.tensor_tensor(d, a, b, ALU.mult))
        sch.op("vector", lambda e, d=h2_3, a=t3: e.tensor_tensor(d, d, a, ALU.add))
    v2 = sch.op("vector", lambda e: e.tensor_tensor(h2, h2, cp["cb2"], ALU.add), inc="v")
    sch.wait("scalar", "v", v2)
    s2 = sch.op("scalar", lambda e: e.activation(h2, h2, AF.Gelu), inc="s")
    sch.wait("vector", "s", s2)
    sch.op("vector", lambda e: e.tensor_tensor(tm3, h2, cp["cw3"], ALU.mult))
    sch.op("vector", lambda e: e.tensor_reduce(
        cwt, tm3.rearrange("h (w j) -> h w j", j=32), AX.X, ALU.add))
    vpad()
    sch.op("vector", lambda e: e.tensor_scalar(cwt, cwt, cp["cb3"], None, ALU.add))
    vpad()
    v3 = sch.op("vector", lambda e: e.reduce_max(csml[:, 0:1], cwt, AX.X, negate=True), inc="v")
    sch.wait("scalar", "v", v3)
    s3 = sch.op("scalar", lambda e: e.activation(cwt, cwt, AF.Exp, bias=csml[:, 0:1],
                                                 accum_out=csml[:, 1:2]), inc="s")
    sch.wait("vector", "s", s3)
    sch.op("vector", lambda e: e.reciprocal(csml[:, 2:3], csml[:, 1:2]))
    vpad()
    v4 = sch.op("vector", lambda e: e.tensor_scalar(cwt, cwt, csml[:, 2:3], None, ALU.mult), inc="v")
    sch.wait("sync", "v", v4)
    st1 = sch.op("sync", lambda e: e.dma_start(out=conn_d, in_=cwt), inc="st", dma=True)
    sch.wait("sync", "st", st1)
    ld1 = sch.op("sync", lambda e: e.dma_start(
        out=cfl[:], in_=conn_d.rearrange("h w -> (h w)").unsqueeze(0)), inc="ld", dma=True)
    sch.wait("tensor", "ld", ld1)
    t1 = sch.op("tensor", lambda e: e.matmul(ps_tr[0][:, 0:H * W7], ones1[:], cfl[:],
                                             start=True, stop=True), inc="t")
    t2 = sch.op("tensor", lambda e: e.matmul(ps_tr[1][:, 0:1], ones1[:], cp["wsv"],
                                             start=True, stop=True), inc="t")
    mm_done(ps_tr[0], t1)
    mm_done(ps_tr[1], t2)
    sch.wait("vector", "t", t2)
    connb = sall[:, 1024:1024 + H * W7]
    v5 = sch.op("vector", lambda e: e.tensor_copy(connb, ps_tr[0][:, 0:H * W7]), inc="v")
    ev_done(ps_tr[0], "v", v5)
    v6 = sch.op("vector", lambda e: e.tensor_copy(ws_b[:], ps_tr[1][:, 0:1]), inc="v")
    ev_done(ps_tr[1], "v", v6)
    sch.wait("vector", "v", init_v)
    cb_done = init_v
    for h in range(H):
        for w in range(W7):
            dmw = dmk[:, w * KW:(w + 1) * KW]
            dst = cbt[:, h * KW:(h + 1) * KW]
            scl = connb[:, h * W7 + w:h * W7 + w + 1]
            cb_done = sch.op("vector",
                             lambda e, dm=dmw, d=dst, s=scl: e.scalar_tensor_tensor(
                                 d, dm, s, d, ALU.mult, ALU.add),
                             inc="v")

    NBITS = 8
    NSEG = 2
    ev_xload = {}
    ev_xfree = {}
    ev_xnf = {}
    ev_kstrip = {}
    ev_vhead = {}
    ev_proj_read = {}
    ev_xn_tr = {}
    ev_xi_tr = {}
    ev_attn_bw = {}

    def ln_quant_chunk(i, ch, gch):
        xs = xslot[gch % 2]
        sch.wait("vector", "ld", ev_xload[gch])
        sch.wait("scalar", "ld", ev_xload[gch])
        sch.op("vector", lambda e: e.reduce_sum(stA[:, 0:1], xs, AX.X))
        vpad()
        sch.wait("scalar", "t", ev_xi_tr.get(gch - 1, 0))
        sv = sch.op("scalar", lambda e: e.activation(xi_tm, xs, AF.Square,
                                                     accum_out=stB[:, 0:1]), inc="s")
        sch.wait("vector", "s", sv)
        sch.op("vector", lambda e: e.tensor_scalar(stB[:, 2:3], stA[:, 0:1], -1.0 / D, None, ALU.mult))
        vpad()
        sch.op("vector", lambda e: e.tensor_tensor(stA[:, 2:3], stB[:, 2:3], stB[:, 2:3], ALU.mult))
        vpad()
        sch.op("vector", lambda e: e.scalar_tensor_tensor(
            stC[:, 0:1], stB[:, 0:1], 1.0 / D, stA[:, 2:3], ALU.mult, ALU.subtract))
        vpad()
        vv = sch.op("vector", lambda e: e.tensor_scalar(
            stA[:, 6:7], stC[:, 0:1], 1.0, 1e-5, ALU.mult, ALU.add), inc="v")
        sch.wait("scalar", "v", vv)
        sch.op("scalar", lambda e: e.activation(stB[:, 6:7], stA[:, 6:7], AF.Ln))
        sv2 = sch.op("scalar", lambda e: e.activation(stC[:, 2:3], stB[:, 6:7], AF.Exp, scale=-0.5), inc="s")
        sch.wait("vector", "s", sv2)
        vv2 = sch.op("vector", lambda e: e.tensor_tensor(stA[:, 8:9], stB[:, 2:3], stC[:, 2:3], ALU.mult), inc="v")
        sch.wait("scalar", "v", vv2)
        sch.wait("scalar", "t", ev_xn_tr.get(gch - 1, 0))
        sxn = sch.op("scalar", lambda e: e.activation(xn_tm, xs, AF.Identity,
                                                      bias=stA[:, 8:9], scale=stC[:, 2:3]), inc="s")
        sch.wait("vector", "s", sxn)
        sch.op("vector", lambda e: e.reduce_max(amax[:, ch:ch + 1], xn_tm, AX.X,
                                                apply_absolute_value=True))
        vpad()
        sch.op("vector", lambda e: e.tensor_scalar(stA[:, 12:13], amax[:, ch:ch + 1], 1e-5, None, ALU.max))
        vpad()
        sch.op("vector", lambda e: e.reciprocal(stB[:, 12:13], stA[:, 12:13]))
        vpad()
        sch.op("vector", lambda e: e.tensor_scalar(sqs[:, gch:gch + 1], stB[:, 12:13], 127.0, None, ALU.mult))
        vv3 = sch.op("vector", lambda e: e.scalar_tensor_tensor(
            stB[:, 8:9], stA[:, 6:7], 1e-6, stC[:, 0:1], ALU.mult, ALU.add), inc="v")
        sch.wait("scalar", "v", vv3)
        sv3 = sch.op("scalar", lambda e: e.activation(stA[:, 10:11], stB[:, 8:9], AF.Ln), inc="s")
        sch.wait("vector", "s", sv3)
        vv4 = sch.op("vector", lambda e: e.tensor_tensor(stC[:, 4:5], stB[:, 6:7], stA[:, 10:11], ALU.subtract), inc="v")
        sch.wait("scalar", "v", vv4)
        sv4 = sch.op("scalar", lambda e: e.activation(stB[:, 10:11], stC[:, 4:5], AF.Exp, scale=0.5), inc="s")
        sch.wait("vector", "s", sv4)
        sch.op("vector", lambda e: e.tensor_tensor(stot[:, ch:ch + 1], stB[:, 10:11], sqs[:, gch:gch + 1], ALU.mult))
        vpad()
        sch.op("vector", lambda e: e.tensor_scalar(xs, xn_tm, stot[:, ch:ch + 1], None, ALU.mult))
        xiv = None
        for qc in range(NSEG):
            seg = slice(qc * 512, (qc + 1) * 512)
            sch.op("vector", lambda e, s=seg: e.tensor_scalar(u_b, xs[:, s], -1.0, -128.5, ALU.mult, ALU.add))
            vpad()
            for b in range(NBITS - 1, -1, -1):
                tb = float(2 ** b)
                vc = sch.op("vector", lambda e, tb=tb: e.tensor_scalar(
                    c_b, u_b, -tb, tb, ALU.is_le, ALU.mult), inc="v")
                sch.wait("gpsimd", "v", vc)
                gw = sch.op("gpsimd", lambda e: e.tensor_tensor(u_b, u_b, c_b, ALU.add), inc="g")
                sch.wait("vector", "g", gw)
            xiv = sch.op("vector", lambda e, s=seg: e.scalar_tensor_tensor(
                xi_tm[:, s], u_b, 0.5, xs[:, s], ALU.add, ALU.add), inc="v")
        ev_xfree[gch] = xiv
        return sxn, xiv

    def transposes_chunk(i, ch, gch, sxn, xiv):
        for b in range(2):
            psb = ps_tr[b]
            mm_guard(psb)
            sch.wait("tensor", "s", sxn)
            for k in range(4):
                dc = b * 4 + k
                tv = sch.op("tensor", lambda e, dc=dc, k=k, psb=psb: e.transpose(
                    psb[:, k * 128:(k + 1) * 128],
                    xn_tm[:, dc * 128:(dc + 1) * 128], ident), inc="t")
            mm_done(psb, tv)
            ev_xn_tr[gch] = tv
            ev_guard("vector", psb)
            dco = b * 4
            hidst = xnf_hi.rearrange("p (c n) -> p c n", n=TA + 6)[:, dco:dco + 4, 3 + ch * 128:3 + ch * 128 + 128]
            lodst = xnf_lo.rearrange("p (c n) -> p c n", n=TA + 6)[:, dco:dco + 4, 3 + ch * 128:3 + ch * 128 + 128]
            ps3 = psb[:, 0:512].rearrange("p (c n) -> p c n", n=128)
            sch.op("vector", lambda e, d=hidst, s=ps3: e.tensor_copy(d, s))
            ve = sch.op("vector", lambda e, d=lodst, s=ps3, h=hidst: e.tensor_tensor(d, s, h, ALU.subtract), inc="v")
            ev_done(psb, "v", ve)
        for b in range(2):
            psb = ps_tr[b]
            mm_guard(psb)
            sch.wait("tensor", "v", xiv)
            for k in range(4):
                dc = b * 4 + k
                tv = sch.op("tensor", lambda e, dc=dc, k=k, psb=psb: e.transpose(
                    psb[:, k * 128:(k + 1) * 128],
                    xi_tm[:, dc * 128:(dc + 1) * 128], ident), inc="t")
            mm_done(psb, tv)
            ev_xi_tr[gch] = tv
            ev_guard("vector", psb)
            dco = b * 4
            dst = xi_fm.rearrange("p (c n) -> p c n", n=128)[:, dco:dco + 4, :]
            ps3 = psb[:, 0:512].rearrange("p (c n) -> p c n", n=128)
            ve = sch.op("vector", lambda e, d=dst, s=ps3: e.tensor_copy(d, s), inc="v")
            ev_done(psb, "v", ve)
        sch.wait("sync", "v", ve)
        sch.op("sync", lambda e, g=gch: e.dma_start(
            out=xint_d[:, :, g * 128:(g + 1) * 128],
            in_=xi_fm.rearrange("p (c n) -> p c n", n=128)), inc="st", dma=True)

    def mm3(psum_ap, ps_handle, st_hi, st_lo, mv_hi, mv_lo, first, last, tp=None):
        sch.op("tensor", lambda e: e.matmul(psum_ap, st_hi, mv_hi, start=first, stop=False, tile_position=tp), inc="t")
        sch.op("tensor", lambda e: e.matmul(psum_ap, st_hi, mv_lo, start=False, stop=False, tile_position=tp), inc="t")
        return sch.op("tensor", lambda e: e.matmul(psum_ap, st_lo, mv_hi, start=False, stop=last, tile_position=tp), inc="t")

    SERIAL = os.environ.get("K_SERIAL") == "1"
    for i in range(NT_A + 1):
        if SERIAL:
            sch.barrier()
        if i < NT_A:
            for ch in range(CH_A):
                gch = i * CH_A + ch
                sch.wait("sync", "v", ev_xfree.get(gch - 2, 0))
                ev_xload[gch] = sch.op("sync", lambda e, g=gch, sl=xslot[gch % 2]: e.dma_start(
                    out=sl, in_=xio.rearrange("(c p) d -> p c d", p=128)[:, g, :]),
                    inc="ld", dma=True)

        if i < NT_A:
            sch.wait("vector", "t", ev_proj_read.get(i - 1, 0))
            x3h = xnf_hi.rearrange("p (c n) -> p c n", n=TA + 6)[:, :, 0:3]
            x3l = xnf_lo.rearrange("p (c n) -> p c n", n=TA + 6)[:, :, 0:3]
            s3h = strip_hi.rearrange("p (c n) -> p c n", n=3)
            s3l = strip_lo.rearrange("p (c n) -> p c n", n=3)
            sch.op("vector", lambda e: e.tensor_copy(x3h, s3h))
            sch.op("vector", lambda e: e.tensor_copy(x3l, s3l))
            for ch in range(CH_A):
                gch = i * CH_A + ch
                sxn, xiv = ln_quant_chunk(i, ch, gch)
                transposes_chunk(i, ch, gch, sxn, xiv)
            ev_xnf[i] = sch.cnt["v"]
            sph = xnf_hi.rearrange("p (c n) -> p c n", n=TA + 6)[:, :, TA:TA + 3]
            spl = xnf_lo.rearrange("p (c n) -> p c n", n=TA + 6)[:, :, TA:TA + 3]
            sch.op("vector", lambda e: e.tensor_copy(s3h, sph))
            sch.op("vector", lambda e: e.tensor_copy(s3l, spl))

            sch.wait("tensor", "v", ev_xnf[i])
            kb = kbuf[i % 2]
            NKC = TA + 3
            for oc in range(NDC):
                psum = ps_pj[oc % 2]
                mm_guard(psum)
                for dc in range(NDC):
                    tv = mm3(psum[:, 0:NKC], psum,
                             wn["wk_hi"][:, dc * D + oc * 128:dc * D + oc * 128 + 128],
                             wn["wk_lo"][:, dc * D + oc * 128:dc * D + oc * 128 + 128],
                             xnf_hi[:, dc * (TA + 6):dc * (TA + 6) + NKC],
                             xnf_lo[:, dc * (TA + 6):dc * (TA + 6) + NKC],
                             dc == 0, dc == NDC - 1)
                mm_done(psum, tv)
                ev_guard("vector", psum)
                ve = sch.op("vector", lambda e, o=oc, p=psum, kk=kb: e.tensor_copy(
                    kk[:, o * (TA + 6):o * (TA + 6) + NKC], p[:, 0:NKC]), inc="v")
                ev_done(psum, "v", ve)
            if _dbga:
                dkv = sch.op("sync", lambda e, t=i, kk=kb: e.dma_start(out=dbg_k2[t], in_=kk), inc="st", dma=True)
                sch.wait("vector", "st", dkv)
            if i >= 1:
                kprev = kbuf[(i - 1) % 2]
                src3 = kb.rearrange("p (c n) -> p c n", n=TA + 6)[:, :, 3:6]
                dst3 = kprev.rearrange("p (c n) -> p c n", n=TA + 6)[:, :, TA + 3:TA + 6]
                ev_kstrip[i - 1] = sch.op("vector", lambda e, s=src3, d=dst3: e.tensor_copy(d, s), inc="v")

            for ob2 in range(2):
                psum = ps_pj[ob2]
                mm_guard(psum)
                for dc in range(NDC):
                    tv = mm3(psum[0:32, :], psum,
                             xnf_hi[:, dc * (TA + 6):dc * (TA + 6) + 32],
                             xnf_lo[:, dc * (TA + 6):dc * (TA + 6) + 32],
                             wn["wv_hi"][:, dc * D + ob2 * 512:dc * D + ob2 * 512 + 512],
                             wn["wv_lo"][:, dc * D + ob2 * 512:dc * D + ob2 * 512 + 512],
                             dc == 0, dc == NDC - 1)
                mm_done(psum, tv)
                ev_guard("vector", psum)
                ve = sch.op("vector", lambda e, o=ob2, p=psum: e.tensor_copy(
                    vhead[0:32, o * 512:(o + 1) * 512], p[0:32, :]), inc="v")
                ev_done(psum, "v", ve)
            ev_vhead[i] = ve
        elif i == NT_A:
            kprev = kbuf[(i - 1) % 2]
            dst3 = kprev.rearrange("p (c n) -> p c n", n=TA + 6)[:, :, TA + 3:TA + 6]
            sch.wait("vector", "t", ev_proj_read.get(i - 1, 0))
            ev_kstrip[i - 1] = sch.op("vector", lambda e, d=dst3: e.memset(d, 0.0), inc="v")
            vz = sch.op("vector", lambda e: e.memset(vhead, 0.0), inc="v")
            sch.wait("tensor", "v", vz)
            for ob2 in range(2):
                psum = ps_pj[ob2]
                mm_guard(psum)
                for dc in range(NDC):
                    tv = mm3(psum[0:3, :], psum,
                             strip_hi[:, dc * 3:dc * 3 + 3],
                             strip_lo[:, dc * 3:dc * 3 + 3],
                             wn["wv_hi"][:, dc * D + ob2 * 512:dc * D + ob2 * 512 + 512],
                             wn["wv_lo"][:, dc * D + ob2 * 512:dc * D + ob2 * 512 + 512],
                             dc == 0, dc == NDC - 1)
                mm_done(psum, tv)
                ev_guard("vector", psum)
                ve = sch.op("vector", lambda e, o=ob2, p=psum: e.tensor_copy(
                    vhead[0:3, o * 512:(o + 1) * 512], p[0:3, :]), inc="v")
                ev_done(psum, "v", ve)
            ev_vhead[i] = ve

        if SERIAL:
            sch.barrier()
        if i >= 1:
            ti = i - 1
            sch.wait("tensor", "v", ev_kstrip[ti])
            sch.wait("tensor", "v", ev_vhead[i])
            sch.wait("tensor", "v", cb_done)
            kprev = kbuf[ti % 2]
            if _dbga:
                sch.op("sync", lambda e, t=ti: e.dma_start(out=dbg_q[t], in_=qbuf), inc="st", dma=True)
                sch.op("sync", lambda e, t=ti, kp=kprev: e.dma_start(out=dbg_k[t], in_=kp), inc="st", dma=True)
                sch.op("sync", lambda e, t=ti: e.dma_start(out=dbg_v[t], in_=vbuf), inc="st", dma=True)
                dbv = sch.op("sync", lambda e, t=ti: e.dma_start(out=dbg_vh[t], in_=vhead), inc="st", dma=True)
                sch.wait("tensor", "st", dbv)
            for ch in range(CH_A):
                gch = ti * CH_A + ch
                for b0, nb in [(0, 3), (3, 3), (6, 2)]:
                    mm_guard(ps_sc[0])
                    mm_guard(ps_sc[1])
                    for pi in range(b0, b0 + nb):
                        sl = pi - b0
                        qs0 = qbuf[0:64, pi * TA + ch * 128:pi * TA + ch * 128 + 128]
                        qs1 = qbuf[64:128, pi * TA + ch * 128:pi * TA + ch * 128 + 128]
                        ks0 = kprev[0:64, pi * (TA + 6) + ch * 128:pi * (TA + 6) + ch * 128 + KW]
                        ks1 = kprev[64:128, pi * (TA + 6) + ch * 128:pi * (TA + 6) + ch * 128 + KW]
                        sch.op("tensor", lambda e, q=qs0, k2=ks0, s=sl: e.matmul(
                            ps_sc[0][:, s * KW:(s + 1) * KW], q, k2, start=True, stop=True,
                            tile_position=(0, 0)), inc="t")
                        tv = sch.op("tensor", lambda e, q=qs1, k2=ks1, s=sl: e.matmul(
                            ps_sc[1][:, s * KW:(s + 1) * KW], q, k2, start=True, stop=True,
                            tile_position=(64, 0)), inc="t")
                    mm_done(ps_sc[0], tv)
                    mm_done(ps_sc[1], tv)
                    for par in range(2):
                        psb = ps_sc[par]
                        ev_guard("vector", psb)
                        h0 = 2 * b0 + par
                        dst = sall.rearrange("p (h n) -> p h n", n=KW)[:, h0:h0 + 2 * nb - 1:2, :]
                        src = psb[:, 0:nb * KW].rearrange("p (s n) -> p s n", n=KW)
                        m1b = m1t.unsqueeze(1).broadcast_to([128, nb, KW])
                        ve = sch.op("vector", lambda e, d=dst, s=src, m=m1b: e.scalar_tensor_tensor(
                            d, s, 0.125, m, ALU.mult, ALU.add), inc="v")
                        ev_done(psb, "v", ve)
                vm = sch.op("vector", lambda e: e.reduce_max(negm[:], sall, AX.X, negate=True), inc="v")
                sch.wait("scalar", "v", vm)
                se = sch.op("scalar", lambda e: e.activation(sall, sall, AF.Exp, bias=negm[:]), inc="s")
                sch.wait("vector", "s", se)
                sch.op("vector", lambda e: e.tensor_reduce(
                    abd_a[:], sall.rearrange("p (h n) -> p h n", n=KW), AX.X, ALU.add))
                for h in range(H):
                    ssl = sall[:, h * KW:(h + 1) * KW]
                    csl = cbt[:, h * KW:(h + 1) * KW]
                    sch.op("vector", lambda e, s=ssl, c=csl, h=h: e.scalar_tensor_tensor(
                        s, s, 1.0, c, ALU.mult, ALU.mult, accum_out=abd_b[:, h:h + 1]))
                vpad()
                sch.op("vector", lambda e: e.scalar_tensor_tensor(
                    abd_d[:], abd_a[:], 1e-9, abd_b[:], ALU.mult, ALU.add))
                vpad()
                sch.op("vector", lambda e: e.reciprocal(abd_i[:], abd_d[:]))
                vpad()
                for h in range(H):
                    ssl = sall[:, h * KW:(h + 1) * KW]
                    vf = sch.op("vector", lambda e, s=ssl, h=h: e.tensor_scalar(
                        s, s, abd_i[:, h:h + 1], None, ALU.mult), inc="v")
                fw_done = vf
                if _dbga:
                    sch.wait("sync", "v", fw_done)
                    dsv = sch.op("sync", lambda e, g=gch: e.dma_start(out=dbg_sall[g], in_=sall), inc="st", dma=True)
                    sch.wait("tensor", "st", dsv)
                ev_qta_read = 0
                for tb in range(4):
                    mm_guard(ps_tr[0])
                    mm_guard(ps_tr[1])
                    sch.wait("tensor", "v", fw_done)
                    for k in range(4):
                        h = tb * 4 + k
                        sch.op("tensor", lambda e, h=h, k=k: e.transpose(
                            ps_tr[0][:, k * 128:(k + 1) * 128],
                            sall[:, h * KW:h * KW + 128], ident), inc="t")
                        tv = sch.op("tensor", lambda e, h=h, k=k: e.transpose(
                            ps_tr[1][0:6, k * 128:(k + 1) * 128],
                            sall[:, h * KW + 128:h * KW + KW], ident), inc="t")
                    mm_done(ps_tr[0], tv)
                    mm_done(ps_tr[1], tv)
                    ev_guard("vector", ps_tr[0])
                    ev_guard("vector", ps_tr[1])
                    sch.wait("vector", "t", ev_qta_read)
                    sch.op("vector", lambda e: e.tensor_copy(qta, ps_tr[0][:, 0:512]))
                    ve = sch.op("vector", lambda e: e.tensor_copy(qtb[0:6, :], ps_tr[1][0:6, 0:512]), inc="v")
                    ev_done(ps_tr[0], "v", ve)
                    ev_done(ps_tr[1], "v", ve)
                    sch.wait("tensor", "v", ve)
                    psb = ps_ou[tb // 2]
                    if tb % 2 == 0:
                        mm_guard(psb)
                    for pp in range(2):
                        p = tb * 2 + pp
                        for hh in range(2):
                            h = 2 * p + hh
                            slot = h % 4
                            v1s = vbuf[:, ch * D + h * 64:ch * D + h * 64 + 64]
                            if ch < CH_A - 1:
                                v2s = vbuf[0:32, (ch + 1) * D + h * 64:(ch + 1) * D + h * 64 + 64]
                            else:
                                v2s = vhead[0:32, h * 64:h * 64 + 64]
                            q1 = qta[:, slot * 128:(slot + 1) * 128]
                            q2 = qtb[0:32, slot * 128:(slot + 1) * 128]
                            po = psb[hh * 64:(hh + 1) * 64, (p % 4) * 128:(p % 4) * 128 + 128]
                            sch.op("tensor", lambda e, v=v1s, q=q1, o=po, hh=hh: e.matmul(
                                o, v, q, start=True, stop=False, tile_position=(0, hh * 64)), inc="t")
                            tv = sch.op("tensor", lambda e, v=v2s, q=q2, o=po, hh=hh: e.matmul(
                                o, v, q, start=False, stop=True, tile_position=(0, hh * 64)), inc="t")
                    mm_done(psb, tv)
                    ev_qta_read = tv
                    if tb % 2 == 1:
                        ev_guard("vector", psb)
                        sch.wait("vector", "st", ev_attn_bw.get(gch, 0))
                        k4 = tb // 2
                        dst = attn_b.rearrange("p (c n) -> p c n", n=128)[:, 0:4, :]
                        ve = sch.op("vector",
                                    lambda e, d=dst, s=psb[:, 0:512].rearrange("p (c n) -> p c n", n=128):
                                    e.tensor_copy(d, s), inc="v")
                        ev_done(psb, "v", ve)
                        sch.wait("sync", "v", ve)
                        sv = sch.op("sync", lambda e, g=gch, k4=k4: e.dma_start(
                            out=attn_d[:, 4 * k4:4 * k4 + 4, g * 128:(g + 1) * 128],
                            in_=attn_b.rearrange("p (c n) -> p c n", n=128)[:, 0:4, :]),
                            inc="st", dma=True)
                        ev_attn_bw[gch + 0] = sv

        if SERIAL:
            sch.barrier()
        if i < NT_A:
            sch.wait("tensor", "v", ev_xnf[i])
            for oc in range(NDC):
                psum = ps_pj[oc % 2]
                mm_guard(psum)
                for dc in range(NDC):
                    tv = mm3(psum[:, 0:TA], psum,
                             wn["wq_hi"][:, dc * D + oc * 128:dc * D + oc * 128 + 128],
                             wn["wq_lo"][:, dc * D + oc * 128:dc * D + oc * 128 + 128],
                             xnf_hi[:, dc * (TA + 6) + 3:dc * (TA + 6) + 3 + TA],
                             xnf_lo[:, dc * (TA + 6) + 3:dc * (TA + 6) + 3 + TA],
                             dc == 0, dc == NDC - 1)
                mm_done(psum, tv)
                ev_guard("vector", psum)
                ve = sch.op("vector", lambda e, o=oc, p=psum: e.tensor_copy(
                    qbuf[:, o * TA:(o + 1) * TA], p[:, 0:TA]), inc="v")
                ev_done(psum, "v", ve)
            for tc in range(CH_A):
                for ob2 in range(2):
                    psum = ps_pj[(tc * 2 + ob2) % 2]
                    mm_guard(psum)
                    for dc in range(NDC):
                        tv = mm3(psum[:], psum,
                                 xnf_hi[:, dc * (TA + 6) + tc * 128:dc * (TA + 6) + tc * 128 + 128],
                                 xnf_lo[:, dc * (TA + 6) + tc * 128:dc * (TA + 6) + tc * 128 + 128],
                                 wn["wv_hi"][:, dc * D + ob2 * 512:dc * D + ob2 * 512 + 512],
                                 wn["wv_lo"][:, dc * D + ob2 * 512:dc * D + ob2 * 512 + 512],
                                 dc == 0, dc == NDC - 1)
                    mm_done(psum, tv)
                    ev_guard("vector", psum)
                    ve = sch.op("vector", lambda e, t=tc, o=ob2, p=psum: e.tensor_copy(
                        vbuf[:, t * D + o * 512:t * D + o * 512 + 512], p[:, 0:512]), inc="v")
                    ev_done(psum, "v", ve)
            ev_proj_read[i] = sch.cnt["t"]

    finA = {s: sch.cnt[s] for s in ("t", "v", "s", "g", "st")}

    for eng in ("sync", "tensor", "vector", "scalar", "gpsimd"):
        for s in ("t", "v", "s", "g"):
            if (eng, s) not in (("tensor", "t"), ("vector", "v"), ("scalar", "s"), ("gpsimd", "g")):
                sch.wait(eng, s, finA[s])
    sch.wait("sync", "st", finA["st"])

    stageB = xb2[:, 0:2 * D]
    last_split = 0
    for piece in range(4):
        dc0 = piece * 2
        sch.wait("sync", "v", last_split)
        lv = sch.op("sync",
                    lambda e, dc0=dc0: e.dma_start(
                        out=stageB,
                        in_=woT.rearrange("(c p) o -> p c o", p=128)[:, dc0:dc0 + 2, :]),
                    inc="ld", dma=True)
        sch.wait("vector", "ld", lv)
        for j in range(2):
            dc = dc0 + j
            src = stageB[:, j * D:(j + 1) * D]
            hi = wn["wq_hi"][:, dc * D:(dc + 1) * D]
            lo = wn["wq_lo"][:, dc * D:(dc + 1) * D]
            sch.op("vector", lambda e, s=src, h=hi: e.tensor_copy(h, s))
            last_split = sch.op("vector",
                                lambda e, s=src, h=hi, l=lo: e.tensor_tensor(l, s, h, ALU.subtract),
                                inc="v")
    lb1 = sch.op("sync", lambda e: e.dma_start(out=bo_t, in_=bo_rep), inc="ld", dma=True)
    lb2 = sch.op("sync", lambda e: e.dma_start(out=bf_t, in_=bf_rep), inc="ld", dma=True)
    sch.wait("vector", "ld", lb2)
    sch.op("vector", lambda e: e.reciprocal(rq_b[:], sqs[:]))
    vpad()
    gsv = sch.op("vector", lambda e: e.tensor_scalar(gs_b[:], rq_b[:], ws_b[:, 0:1], None, ALU.mult), inc="v")
    wo_ready = last_split
    sch.wait("sync", "v", wo_ready)

    ev_bld = {}
    ev_bfree = {}
    ev_outdma = {}
    ev_mm_read = 0

    def emit_b_loads(j):
        t0 = j * TB
        par = j % 2
        sch.wait("sync", "st", ev_outdma.get(j - 1, 0))
        sch.wait("sync", "v", ev_bfree.get(j - 1, 0))
        l1 = sch.op("sync", lambda e, t0=t0: e.dma_start(
            out=ab_buf.rearrange("p (c n) -> p c n", n=TB),
            in_=attn_d[:, :, t0:t0 + TB]), inc="ld", dma=True)
        l2 = sch.op("sync", lambda e, t0=t0, p=par: e.dma_start(
            out=xib[p].rearrange("p (c n) -> p c n", n=TB),
            in_=xint_d[:, :, t0:t0 + TB]), inc="ld", dma=True)
        l3 = sch.op("sync", lambda e, t0=t0: e.dma_start(
            out=xb2.rearrange("p (c n) -> p c n", n=D),
            in_=xio.rearrange("(c p) d -> p c d", p=128)[:, t0 // 128:t0 // 128 + CH_B, :]),
            inc="ld", dma=True)
        ev_bld[j] = (l1, l2, l3)

    emit_b_loads(0)
    for j in range(1, NT_B + 1):
        jj = j - 1
        par = jj % 2
        t0 = jj * TB
        l1, l2, l3 = ev_bld[jj]
        sch.wait("vector", "ld", l1)
        sch.wait("vector", "t", ev_mm_read)
        for dc in range(NDC):
            src = ab_buf[:, dc * TB:(dc + 1) * TB]
            hi = ah_b[:, dc * TB:(dc + 1) * TB]
            lo = al_b[:, dc * TB:(dc + 1) * TB]
            sch.op("vector", lambda e, s=src, h=hi: e.tensor_copy(h, s))
            vsp = sch.op("vector", lambda e, s=src, h=hi, l=lo: e.tensor_tensor(l, s, h, ALU.subtract), inc="v")
        s_tm = ab_buf
        sch.wait("tensor", "v", vsp)
        for tc in range(CH_B):
            for ob2 in range(2):
                psum = ps_pj[(tc * 2 + ob2) % 2]
                mm_guard(psum)
                for dc in range(NDC):
                    tv = mm3(psum[:], psum,
                             ah_b[:, dc * TB + tc * 128:dc * TB + tc * 128 + 128],
                             al_b[:, dc * TB + tc * 128:dc * TB + tc * 128 + 128],
                             wn["wq_hi"][:, dc * D + ob2 * 512:dc * D + ob2 * 512 + 512],
                             wn["wq_lo"][:, dc * D + ob2 * 512:dc * D + ob2 * 512 + 512],
                             dc == 0, dc == NDC - 1)
                mm_done(psum, tv)
                ev_guard("vector", psum)
                ve = sch.op("vector", lambda e, t=tc, o=ob2, p=psum: e.scalar_tensor_tensor(
                    s_tm[:, t * D + o * 512:t * D + o * 512 + 512], p[:], 1.0,
                    bo_t[:, o * 512:(o + 1) * 512], ALU.mult, ALU.add), inc="v")
                ev_done(psum, "v", ve)
        sch.wait("tensor", "ld", l2)
        sch.wait("tensor", "v", gsv)
        for tc in range(CH_B):
            gch = jj * CH_B + tc
            for ob2 in range(2):
                psum = ps_ou[ob2]
                mm_guard(psum)
                for dc in range(NDC):
                    tv = sch.op("tensor", lambda e, dc=dc, p=psum, t=tc, o=ob2, pr=par: e.matmul(
                        p[:], xib[pr][:, dc * TB + t * 128:dc * TB + t * 128 + 128],
                        wi8_sb[:, dc * D + o * 512:dc * D + o * 512 + 512],
                        start=(dc == 0), stop=(dc == NDC - 1)), inc="t")
                mm_done(psum, tv)
                ev_guard("vector", psum)
                vg = sch.op("vector", lambda e, t=tc, o=ob2, p=psum, g=gch: e.scalar_tensor_tensor(
                    g_tm[:, t * D + o * 512:t * D + o * 512 + 512], p[:], gs_b[:, g:g + 1],
                    bf_t[:, o * 512:(o + 1) * 512], ALU.mult, ALU.add), inc="v")
                ev_done(psum, "v", vg)
            sch.wait("scalar", "v", vg)
            sg = sch.op("scalar", lambda e, t=tc: e.activation(
                g_tm[:, t * D:(t + 1) * D], g_tm[:, t * D:(t + 1) * D], AF.Silu), inc="s")
            sch.wait("vector", "s", sg)
            sch.wait("vector", "ld", l3)
            sch.op("vector", lambda e, t=tc: e.tensor_tensor(
                g_tm[:, t * D:(t + 1) * D], g_tm[:, t * D:(t + 1) * D],
                s_tm[:, t * D:(t + 1) * D], ALU.mult))
            sch.op("vector", lambda e, t=tc: e.tensor_tensor(
                xb2[:, t * D:(t + 1) * D], xb2[:, t * D:(t + 1) * D],
                g_tm[:, t * D:(t + 1) * D], ALU.add))
        ev_mm_read = sch.cnt["t"]
        vfree = sch.op("vector", lambda e: e.engine_nop(), inc="v")
        ev_bfree[jj] = vfree
        sch.wait("sync", "v", vfree)
        ev_outdma[jj] = sch.op("sync", lambda e, t0=t0: e.dma_start(
            out=out_d.rearrange("(c p) d -> p c d", p=128)[:, t0 // 128:t0 // 128 + CH_B, :],
            in_=xb2.rearrange("p (c n) -> p c n", n=D)), inc="st", dma=True)
        if j < NT_B:
            emit_b_loads(j)

    sch.wait("sync", "st", sch.cnt["st"])

    with nc.Block() as block:
        @block.sync
        def _(sync):
            for fn in sch.prog["sync"]:
                fn(sync)

        @block.tensor
        def _(tensor):
            for fn in sch.prog["tensor"]:
                fn(tensor)

        @block.vector
        def _(vector):
            for fn in sch.prog["vector"]:
                fn(vector)

        @block.scalar
        def _(scalar):
            for fn in sch.prog["scalar"]:
                fn(scalar)

        @block.gpsimd
        def _(gpsimd):
            for fn in sch.prog["gpsimd"]:
                fn(gpsimd)

    return nc


def _host_consts():
    f32 = np.float32
    ident = np.eye(128, dtype=f32)
    ones1 = np.ones((1, 128), dtype=f32)
    t_idx = np.arange(128)[:, None]
    k_idx = np.arange(KW)[None, :]
    band = (k_idx - t_idx >= 0) & (k_idx - t_idx <= 6)
    m1 = np.where(band, 0.0, -np.inf).astype(f32)
    dmask = np.zeros((128, W7 * KW), dtype=f32)
    for w in range(W7):
        dmask[:, w * KW:(w + 1) * KW] = (k_idx - t_idx == w).astype(f32)
    return ident, ones1, m1, dmask


def kernel(L0_attn_x, L0_attn_wq, L0_attn_wk, L0_attn_wv, L0_attn_wo,
           L0_attn_bo, L0_attn_cw1, L0_attn_cb1, L0_attn_cw2, L0_attn_cb2,
           L0_attn_cw3, L0_attn_cb3, L0_ffn_w_int8, L0_ffn_weight_scale,
           L0_ffn_bias):
    x = np.ascontiguousarray(np.asarray(L0_attn_x, dtype=np.float32))
    B, S, d = x.shape
    assert d == D
    f32 = np.float32

    wqT = np.ascontiguousarray(np.asarray(L0_attn_wq, f32).T)
    wkT = np.ascontiguousarray(np.asarray(L0_attn_wk, f32).T)
    wvT = np.ascontiguousarray(np.asarray(L0_attn_wv, f32).T)
    woT = np.ascontiguousarray(np.asarray(L0_attn_wo, f32).T)
    wi8T = np.ascontiguousarray(np.asarray(L0_ffn_w_int8, np.int32).T)
    bo_rep = np.ascontiguousarray(np.broadcast_to(np.asarray(L0_attn_bo, f32), (128, D)))
    bf_rep = np.ascontiguousarray(np.broadcast_to(np.asarray(L0_ffn_bias, f32), (128, D)))
    ws = np.asarray(L0_ffn_weight_scale, f32).reshape(1, 1)

    pos = np.linspace(0.0, 1.0, W7, dtype=f32)
    pos_t = np.ascontiguousarray(np.broadcast_to(pos[None, :, None], (H, W7, 32)).reshape(H, 224))
    cw1 = np.asarray(L0_attn_cw1, f32)
    cb1 = np.asarray(L0_attn_cb1, f32)
    cw2 = np.asarray(L0_attn_cw2, f32)
    cb2 = np.asarray(L0_attn_cb2, f32)
    cw3 = np.asarray(L0_attn_cw3, f32)
    cb3 = np.asarray(L0_attn_cb3, f32)
    cw1r = np.ascontiguousarray(np.broadcast_to(cw1[:, None, :, 0], (H, W7, 32)).reshape(H, 224))
    cb1r = np.ascontiguousarray(np.broadcast_to(cb1[:, None, :], (H, W7, 32)).reshape(H, 224))
    cw2r = np.ascontiguousarray(cw2.reshape(H, 1024))
    cb2r = np.ascontiguousarray(np.broadcast_to(cb2[:, None, :], (H, W7, 32)).reshape(H, 224))
    cw3r = np.ascontiguousarray(np.broadcast_to(cw3, (H, W7, 32)).reshape(H, 224))
    cb3c = np.ascontiguousarray(cb3)

    ident, ones1, m1, dmask = _host_consts()

    nc = build_program(S)
    shared = dict(wqT=wqT, wkT=wkT, wvT=wvT, woT=woT, wi8T=wi8T,
                  bo_rep=bo_rep, bf_rep=bf_rep, ws=ws,
                  pos_t=pos_t, cw1r=cw1r, cb1r=cb1r, cw2r=cw2r, cb2r=cb2r,
                  cw3r=cw3r, cb3c=cb3c, ident=ident, ones1=ones1,
                  m1=m1, dmask=dmask)
    in_maps = [dict(shared, x=np.ascontiguousarray(x[b])) for b in range(B)]
    res = run_bass_kernel_spmd(nc, in_maps, core_ids=list(range(B)))
    if os.environ.get("K_DEBUG_SCRATCH") == "1":
        keys = ["attn_scr", "xint_scr", "conn_scr"]
        if os.environ.get("K_DEBUG_ATTN") == "1":
            keys += ["dbg_q", "dbg_k", "dbg_v", "dbg_vh", "dbg_sall", "dbg_k2"]
        for b in range(B):
            for k in keys:
                np.save(f"/tmp/k_dbg_{k}_{b}.npy", res.results[b][k])
    out = np.stack([r["out"] for r in res.results], axis=0)
    return out.astype(np.float32)


# version 06
# speedup vs baseline: 1.3565x; 1.3565x over previous
import os
import numpy as np
import ml_dtypes
import concourse.bass as bass
import concourse.mybir as mybir
from concourse.bass_utils import run_bass_kernel_spmd

F32 = mybir.dt.float32
BF16 = mybir.dt.bfloat16
I32 = mybir.dt.int32
AF = mybir.ActivationFunctionType
ALU = mybir.AluOpType
AX = mybir.AxisListType

D = 1024
H = 16
NDC = D // 128
TA = 256
TB = 512
W7 = 7
KW = 134


class Sched:
    ENGS = ("sync", "tensor", "vector", "scalar", "gpsimd")

    def __init__(self, nc):
        self.nc = nc
        self.prog = {e: [] for e in self.ENGS}
        self.sems = {}
        self.cnt = {}

    def sem(self, name):
        self.sems[name] = self.nc.alloc_semaphore(name)
        self.cnt[name] = 0

    def op(self, eng, fn, inc=None, dma=False):
        incs = []
        ret = None
        if inc is not None:
            k = 16 if dma else 1
            self.cnt[inc] += k
            ret = self.cnt[inc]
            incs.append((self.sems[inc], k))

        def run(e, fn=fn, incs=incs):
            inst = fn(e)
            for h, k in incs:
                inst.then_inc(h, k)
        self.prog[eng].append(run)
        return ret

    def wait(self, eng, sem, val):
        if val is None or val <= 0:
            return
        h = self.sems[sem]
        self.prog[eng].append(lambda e, h=h, v=val: e.wait_ge(h, v))

    def barrier(self):
        snap = {s: self.cnt[s] for s in ("t", "v", "s", "g", "ld", "st")}
        own = {"tensor": "t", "vector": "v", "scalar": "s", "gpsimd": "g"}
        for eng in self.ENGS:
            for s, v in snap.items():
                if own.get(eng) != s:
                    self.wait(eng, s, v)


def build_program(S):
    assert S % TB == 0
    NT_A = S // TA
    NT_B = S // TB
    CH_A = TA // 128
    CH_B = TB // 128
    NCHUNK = S // 128

    nc = bass.Bass("TRN2", target_bir_lowering=False)

    xio = nc.dram_tensor("x", [S, D], F32, kind="ExternalInput").ap()
    wqT = nc.dram_tensor("wqT", [D, D], F32, kind="ExternalInput").ap()
    wkT = nc.dram_tensor("wkT", [D, D], F32, kind="ExternalInput").ap()
    wvT = nc.dram_tensor("wvT", [D, D], F32, kind="ExternalInput").ap()
    woT = nc.dram_tensor("woT", [D, D], F32, kind="ExternalInput").ap()
    wi8T = nc.dram_tensor("wi8T", [D, D], I32, kind="ExternalInput").ap()
    bo_rep = nc.dram_tensor("bo_rep", [128, D], F32, kind="ExternalInput").ap()
    bf_rep = nc.dram_tensor("bf_rep", [128, D], F32, kind="ExternalInput").ap()
    ws_in = nc.dram_tensor("ws", [1, 1], F32, kind="ExternalInput").ap()
    pos_c = nc.dram_tensor("pos_t", [H, 224], F32, kind="ExternalInput").ap()
    cw1_c = nc.dram_tensor("cw1r", [H, 224], F32, kind="ExternalInput").ap()
    cb1_c = nc.dram_tensor("cb1r", [H, 224], F32, kind="ExternalInput").ap()
    cw2_c = nc.dram_tensor("cw2r", [H, 1024], F32, kind="ExternalInput").ap()
    cb2_c = nc.dram_tensor("cb2r", [H, 224], F32, kind="ExternalInput").ap()
    cw3_c = nc.dram_tensor("cw3r", [H, 224], F32, kind="ExternalInput").ap()
    cb3_c = nc.dram_tensor("cb3c", [H, 1], F32, kind="ExternalInput").ap()
    ident_c = nc.dram_tensor("ident", [128, 128], F32, kind="ExternalInput").ap()
    ones1_c = nc.dram_tensor("ones1", [1, 128], F32, kind="ExternalInput").ap()
    m1_c = nc.dram_tensor("m1", [128, KW], F32, kind="ExternalInput").ap()
    dmask_c = nc.dram_tensor("dmask", [128, W7 * KW], F32, kind="ExternalInput").ap()

    out_d = nc.dram_tensor("out", [S, D], F32, kind="ExternalOutput").ap()
    _dbga = os.environ.get("K_DEBUG_ATTN") == "1"
    if _dbga:
        NT_A_ = S // TA
        dbg_q = nc.dram_tensor("dbg_q", [NT_A_, 128, NDC * TA], F32, kind="ExternalOutput").ap()
        dbg_k = nc.dram_tensor("dbg_k", [NT_A_, 128, NDC * (TA + 6)], F32, kind="ExternalOutput").ap()
        dbg_v = nc.dram_tensor("dbg_v", [NT_A_, 128, CH_A * D], F32, kind="ExternalOutput").ap()
        dbg_vh = nc.dram_tensor("dbg_vh", [NT_A_, 128, D], F32, kind="ExternalOutput").ap()
        dbg_k2 = nc.dram_tensor("dbg_k2", [NT_A_, 128, NDC * (TA + 6)], F32, kind="ExternalOutput").ap()
        dbg_sall = nc.dram_tensor("dbg_sall", [S // 128, 128, H * KW], F32, kind="ExternalOutput").ap()
    _dbg = os.environ.get("K_DEBUG_SCRATCH") == "1"
    _scr = "ExternalOutput" if _dbg else "Internal"
    attn_d = nc.dram_tensor("attn_scr", [128, NDC, S], F32, kind=_scr).ap()
    xint_d = nc.dram_tensor("xint_scr", [128, NDC, S], BF16, kind=_scr).ap()
    conn_d = nc.dram_tensor("conn_scr", [H, W7], F32, kind=_scr).ap()

    sbt = lambda name, shape, dt=F32: nc.alloc_sbuf_tensor(name, shape, dt)

    wn = {}
    for n in ("wq", "wk", "wv"):
        wn[n + "_hi"] = sbt(f"{n}_hi", [128, NDC * D], BF16)
        wn[n + "_lo"] = sbt(f"{n}_lo", [128, NDC * D], BF16)
    wi8_sb = sbt("wi8", [128, NDC * D], BF16)

    offs = {}
    cur = [0]

    def asl_mark(name, n, align=8):
        o = (cur[0] + align - 1) // align * align
        offs[name] = (o, n)
        cur[0] = o + n
        return o

    for nm, n in [("xs0", D), ("xs1", D), ("xn_tm", D),
                  ("qbuf", NDC * TA), ("kb0", NDC * (TA + 6)), ("kb1", NDC * (TA + 6)),
                  ("vbuf", CH_A * D), ("vhead", D), ("sall", H * KW),
                  ("qta", 512), ("qtb", 512), ("attn_b", 512),
                  ("u_b", 512), ("c_b", 512), ("xi_tm", D), ("cbt", H * KW),
                  ("m1t", KW), ("ident", 128)]:
        asl_mark(nm, n)
    A_END = cur[0]
    B_layout = [("ab", CH_B * D), ("xb2", CH_B * D), ("g_tm", CH_B * D),
                ("bo_t", D), ("bf_t", D)]
    bo = 0
    boffs = {}
    for nm, n in B_layout:
        boffs[nm] = (bo, n)
        bo += n
    AR_F32 = max(A_END, bo) + 16
    ar = sbt("arena_f32", [128, AR_F32], F32)

    def A(nm):
        o, n = offs[nm]
        return ar[:, o:o + n]

    def Bs(nm):
        o, n = boffs[nm]
        return ar[:, o:o + n]

    xslot = [A("xs0"), A("xs1")]
    xn_tm = A("xn_tm")
    qbuf = A("qbuf")
    kbuf = [A("kb0"), A("kb1")]
    vbuf = A("vbuf")
    vhead = A("vhead")
    sall = A("sall")
    qta = A("qta")
    qtb = A("qtb")
    attn_b = A("attn_b")
    u_b = A("u_b")
    c_b = A("c_b")
    xi_tm = A("xi_tm")
    cbt = A("cbt")
    m1t = A("m1t")
    ident = A("ident")
    dmk = ar[:, offs["qbuf"][0]:offs["qbuf"][0] + W7 * KW]
    stage = ar[:, offs["kb1"][0]:offs["kb1"][0] + 2 * D]
    cp_base = offs["kb0"][0]
    vb_base = offs["vbuf"][0]

    ab_buf = Bs("ab")
    xb2 = Bs("xb2")
    g_tm = Bs("g_tm")
    bo_t = Bs("bo_t")
    bf_t = Bs("bf_t")

    arb = sbt("arena_b16", [128, 2 * NDC * (TA + 6) + D + NDC * 128 + 32], BF16)
    ob = 0
    xnf_hi = arb[:, ob:ob + NDC * (TA + 6)]; ob += NDC * (TA + 6)
    xnf_lo = arb[:, ob:ob + NDC * (TA + 6)]; ob += NDC * (TA + 6)
    xi_fm = arb[:, ob:ob + NDC * 128]; ob += NDC * 128

    ah_b = wn["wk_hi"][:, 0:NDC * TB]
    al_b = wn["wk_lo"][:, 0:NDC * TB]
    xib = [wn["wv_hi"][:, 0:NDC * TB], wn["wv_lo"][:, 0:NDC * TB]]

    padA = sbt("padA", [128, 8], F32)
    padB = sbt("padB", [128, 8], F32)
    stA = sbt("stA", [128, 16], F32)
    stB = sbt("stB", [128, 16], F32)
    stC = sbt("stC", [128, 16], F32)
    amax = sbt("amax", [128, CH_A], F32)
    sqs = sbt("sqs", [128, NCHUNK], F32)
    stot = sbt("stot", [128, CH_A], F32)
    negm = sbt("negm", [128, 1], F32)
    abd_a = sbt("abd_a", [128, H], F32)
    abd_b = sbt("abd_b", [128, H], F32)
    abd_d = sbt("abd_d", [128, H], F32)
    abd_i = sbt("abd_i", [128, H], F32)
    gs_b = sbt("gs_b", [128, NCHUNK], F32)
    rq_b = sbt("rq_b", [128, NCHUNK], F32)
    ws_b = sbt("ws_b", [128, 1], F32)
    strip_hi = sbt("strip_hi", [128, NDC * 3], BF16)
    strip_lo = sbt("strip_lo", [128, NDC * 3], BF16)
    ident_b = sbt("ident_b", [128, 128], BF16)
    ones1 = sbt("ones1t", [1, 128], F32)
    cfl = sbt("cflat", [1, H * W7], F32)

    ps_tr = [nc.alloc_psum_tensor(f"ps_tr{i}", [128, 512], F32) for i in range(2)]
    ps_pj = [nc.alloc_psum_tensor(f"ps_pj{i}", [128, 512], F32) for i in range(2)]
    ps_sc = [nc.alloc_psum_tensor(f"ps_sc{i}", [128, 3 * KW], F32) for i in range(2)]
    ps_ou = [nc.alloc_psum_tensor(f"ps_ou{i}", [128, 512], F32) for i in range(2)]

    sch = Sched(nc)
    for s in ("ld", "st", "t", "v", "s", "g"):
        sch.sem(s)

    bank_ev = {}
    bank_mm = {}

    def key(ps):
        return id(ps)

    def mm_guard(ps):
        ev = bank_ev.get(key(ps))
        if ev:
            sch.wait("tensor", ev[0], ev[1])

    def mm_done(ps, tval):
        bank_mm[key(ps)] = tval

    def ev_guard(eng, ps):
        sch.wait(eng, "t", bank_mm.get(key(ps), 0))

    def ev_done(ps, sem, val):
        bank_ev[key(ps)] = (sem, val)

    def vpad():
        sch.op("vector", lambda e: e.tensor_copy(padB[:], padA[:]))

    for dst, src in [(ident, ident_c), (ones1[:], ones1_c), (m1t, m1_c),
                     (dmk, dmask_c)]:
        sch.op("sync", lambda e, d=dst, s=src: e.dma_start(out=d, in_=s),
               inc="ld", dma=True)
    cp = {}
    cpo = 0
    for nm, src, n in [("pos", pos_c, 224), ("cw1", cw1_c, 224), ("cb1", cb1_c, 224),
                       ("cw2", cw2_c, 1024), ("cb2", cb2_c, 224)]:
        cp[nm] = ar[0:H, cp_base + cpo:cp_base + cpo + n]
        sch.op("sync", lambda e, d=cp[nm], s=src: e.dma_start(out=d, in_=s),
               inc="ld", dma=True)
        cpo += n
    cpo = 0
    for nm, src, n in [("cw3", cw3_c, 224), ("cb3", cb3_c, 1)]:
        cp[nm] = ar[0:H, vb_base + cpo:vb_base + cpo + n]
        sch.op("sync", lambda e, d=cp[nm], s=src: e.dma_start(out=d, in_=s),
               inc="ld", dma=True)
        cpo += n
    cp["wsv"] = ar[0:1, vb_base + 256:vb_base + 257]
    sch.op("sync", lambda e: e.dma_start(out=cp["wsv"], in_=ws_in), inc="ld", dma=True)
    ld_consts = sch.cnt["ld"]

    last_split = 0
    for wname, wsrc in (("wq", wqT), ("wk", wkT), ("wv", wvT)):
        for piece in range(4):
            dc0 = piece * 2
            sch.wait("sync", "v", last_split)
            lv = sch.op("sync",
                        lambda e, w=wsrc, dc0=dc0: e.dma_start(
                            out=stage,
                            in_=w.rearrange("(c p) o -> p c o", p=128)[:, dc0:dc0 + 2, :]),
                        inc="ld", dma=True)
            sch.wait("vector", "ld", lv)
            for j in range(2):
                dc = dc0 + j
                src = stage[:, j * D:(j + 1) * D]
                hi = wn[wname + "_hi"][:, dc * D:(dc + 1) * D]
                lo = wn[wname + "_lo"][:, dc * D:(dc + 1) * D]
                sch.op("vector", lambda e, s=src, h=hi: e.tensor_copy(h, s))
                last_split = sch.op("vector",
                                    lambda e, s=src, h=hi, l=lo: e.tensor_tensor(l, s, h, ALU.subtract),
                                    inc="v")
    for piece in range(4):
        dc0 = piece * 2
        sch.wait("gpsimd", "v", last_split)
        lv = sch.op("gpsimd",
                    lambda e, dc0=dc0: e.dma_start(
                        out=stage,
                        in_=wi8T.rearrange("(c p) o -> p c o", p=128)[:, dc0:dc0 + 2, :]),
                    inc="ld", dma=True)
        sch.wait("vector", "ld", lv)
        for j in range(2):
            dc = dc0 + j
            src = stage[:, j * D:(j + 1) * D]
            dst = wi8_sb[:, dc * D:(dc + 1) * D]
            last_split = sch.op("vector", lambda e, s=src, d=dst: e.tensor_copy(d, s),
                                inc="v")

    sch.wait("vector", "ld", ld_consts)
    sch.op("vector", lambda e: e.memset(padA[:], 0.0))
    sch.op("vector", lambda e: e.tensor_copy(ident_b[:], ident))
    sch.op("vector", lambda e: e.memset(qtb, 0.0))
    sch.op("vector", lambda e: e.memset(strip_hi[:], 0.0))
    sch.op("vector", lambda e: e.memset(strip_lo[:], 0.0))
    init_v = sch.op("vector", lambda e: e.memset(cbt, 0.0), inc="v")

    h1 = sall[0:H, 0:224]
    h2 = sall[0:H, 256:480]
    tm3 = sall[0:H, 512:736]
    cwt = sall[0:H, 768:775]
    csml = sall[0:H, 800:808]

    sch.op("vector", lambda e: e.tensor_tensor(h1, cp["pos"], cp["cw1"], ALU.mult))
    v1 = sch.op("vector", lambda e: e.tensor_tensor(h1, h1, cp["cb1"], ALU.add), inc="v")
    sch.wait("scalar", "v", v1)
    s1 = sch.op("scalar", lambda e: e.activation(h1, h1, AF.Gelu), inc="s")
    sch.wait("vector", "s", s1)
    sch.op("vector", lambda e: e.memset(h2, 0.0))
    for i in range(32):
        h1_i = h1.rearrange("h (w i) -> h w i", i=32)[:, :, i:i + 1].broadcast_to([H, W7, 32])
        cw2_b = cp["cw2"].rearrange("h (j i) -> h i j", i=32)[:, i:i + 1, :].broadcast_to([H, W7, 32])
        t3 = tm3.rearrange("h (w j) -> h w j", j=32)
        h2_3 = h2.rearrange("h (w j) -> h w j", j=32)
        sch.op("vector", lambda e, a=h1_i, b=cw2_b, d=t3: e.tensor_tensor(d, a, b, ALU.mult))
        sch.op("vector", lambda e, d=h2_3, a=t3: e.tensor_tensor(d, d, a, ALU.add))
    v2 = sch.op("vector", lambda e: e.tensor_tensor(h2, h2, cp["cb2"], ALU.add), inc="v")
    sch.wait("scalar", "v", v2)
    s2 = sch.op("scalar", lambda e: e.activation(h2, h2, AF.Gelu), inc="s")
    sch.wait("vector", "s", s2)
    sch.op("vector", lambda e: e.tensor_tensor(tm3, h2, cp["cw3"], ALU.mult))
    sch.op("vector", lambda e: e.tensor_reduce(
        cwt, tm3.rearrange("h (w j) -> h w j", j=32), AX.X, ALU.add))
    vpad()
    sch.op("vector", lambda e: e.tensor_scalar(cwt, cwt, cp["cb3"], None, ALU.add))
    vpad()
    v3 = sch.op("vector", lambda e: e.reduce_max(csml[:, 0:1], cwt, AX.X, negate=True), inc="v")
    sch.wait("scalar", "v", v3)
    s3 = sch.op("scalar", lambda e: e.activation(cwt, cwt, AF.Exp, bias=csml[:, 0:1],
                                                 accum_out=csml[:, 1:2]), inc="s")
    sch.wait("vector", "s", s3)
    sch.op("vector", lambda e: e.reciprocal(csml[:, 2:3], csml[:, 1:2]))
    vpad()
    v4 = sch.op("vector", lambda e: e.tensor_scalar(cwt, cwt, csml[:, 2:3], None, ALU.mult), inc="v")
    sch.wait("sync", "v", v4)
    st1 = sch.op("sync", lambda e: e.dma_start(out=conn_d, in_=cwt), inc="st", dma=True)
    sch.wait("sync", "st", st1)
    ld1 = sch.op("sync", lambda e: e.dma_start(
        out=cfl[:], in_=conn_d.rearrange("h w -> (h w)").unsqueeze(0)), inc="ld", dma=True)
    sch.wait("tensor", "ld", ld1)
    t1 = sch.op("tensor", lambda e: e.matmul(ps_tr[0][:, 0:H * W7], ones1[:], cfl[:],
                                             start=True, stop=True), inc="t")
    t2 = sch.op("tensor", lambda e: e.matmul(ps_tr[1][:, 0:1], ones1[:], cp["wsv"],
                                             start=True, stop=True), inc="t")
    mm_done(ps_tr[0], t1)
    mm_done(ps_tr[1], t2)
    sch.wait("vector", "t", t2)
    connb = sall[:, 1024:1024 + H * W7]
    v5 = sch.op("vector", lambda e: e.tensor_copy(connb, ps_tr[0][:, 0:H * W7]), inc="v")
    ev_done(ps_tr[0], "v", v5)
    v6 = sch.op("vector", lambda e: e.tensor_copy(ws_b[:], ps_tr[1][:, 0:1]), inc="v")
    ev_done(ps_tr[1], "v", v6)
    sch.wait("vector", "v", init_v)
    cb_done = init_v
    for h in range(H):
        for w in range(W7):
            dmw = dmk[:, w * KW:(w + 1) * KW]
            dst = cbt[:, h * KW:(h + 1) * KW]
            scl = connb[:, h * W7 + w:h * W7 + w + 1]
            cb_done = sch.op("vector",
                             lambda e, dm=dmw, d=dst, s=scl: e.scalar_tensor_tensor(
                                 d, dm, s, d, ALU.mult, ALU.add),
                             inc="v")

    NBITS = 8
    NSEG = 2
    ev_xload = {}
    ev_xfree = {}
    ev_xnf = {}
    ev_kstrip = {}
    ev_vhead = {}
    ev_proj_read = {}
    ev_xn_tr = {}
    ev_xi_tr = {}
    ev_attn_bw = {}

    def ln_core(i, ch, gch):
        xs = xslot[gch % 2]
        sch.wait("vector", "ld", ev_xload[gch])
        sch.wait("scalar", "ld", ev_xload[gch])
        sch.op("vector", lambda e: e.reduce_sum(stA[:, 0:1], xs, AX.X))
        vpad()
        sch.wait("scalar", "t", ev_xi_tr.get(gch - 1, 0))
        sv = sch.op("scalar", lambda e: e.activation(xi_tm, xs, AF.Square,
                                                     accum_out=stB[:, 0:1]), inc="s")
        sch.wait("vector", "s", sv)
        sch.op("vector", lambda e: e.tensor_scalar(stB[:, 2:3], stA[:, 0:1], -1.0 / D, None, ALU.mult))
        vpad()
        sch.op("vector", lambda e: e.tensor_tensor(stA[:, 2:3], stB[:, 2:3], stB[:, 2:3], ALU.mult))
        vpad()
        sch.op("vector", lambda e: e.scalar_tensor_tensor(
            stC[:, 0:1], stB[:, 0:1], 1.0 / D, stA[:, 2:3], ALU.mult, ALU.subtract))
        vpad()
        vv = sch.op("vector", lambda e: e.tensor_scalar(
            stA[:, 6:7], stC[:, 0:1], 1.0, 1e-5, ALU.mult, ALU.add), inc="v")
        sch.wait("scalar", "v", vv)
        sch.op("scalar", lambda e: e.activation(stB[:, 6:7], stA[:, 6:7], AF.Ln))
        sv2 = sch.op("scalar", lambda e: e.activation(stC[:, 2:3], stB[:, 6:7], AF.Exp, scale=-0.5), inc="s")
        sch.wait("vector", "s", sv2)
        vv2 = sch.op("vector", lambda e: e.tensor_tensor(stA[:, 8:9], stB[:, 2:3], stC[:, 2:3], ALU.mult), inc="v")
        sch.wait("scalar", "v", vv2)
        sch.wait("scalar", "t", ev_xn_tr.get(gch - 1, 0))
        sxn = sch.op("scalar", lambda e: e.activation(xn_tm, xs, AF.Identity,
                                                      bias=stA[:, 8:9], scale=stC[:, 2:3]), inc="s")
        sch.wait("vector", "s", sxn)
        sch.op("vector", lambda e: e.reduce_max(amax[:, ch:ch + 1], xn_tm, AX.X,
                                                apply_absolute_value=True))
        vpad()
        sch.op("vector", lambda e: e.tensor_scalar(stA[:, 12:13], amax[:, ch:ch + 1], 1e-5, None, ALU.max))
        vpad()
        sch.op("vector", lambda e: e.reciprocal(stB[:, 12:13], stA[:, 12:13]))
        vpad()
        sch.op("vector", lambda e: e.tensor_scalar(sqs[:, gch:gch + 1], stB[:, 12:13], 127.0, None, ALU.mult))
        vv3 = sch.op("vector", lambda e: e.scalar_tensor_tensor(
            stB[:, 8:9], stA[:, 6:7], 1e-6, stC[:, 0:1], ALU.mult, ALU.add), inc="v")
        sch.wait("scalar", "v", vv3)
        sv3 = sch.op("scalar", lambda e: e.activation(stA[:, 10:11], stB[:, 8:9], AF.Ln), inc="s")
        sch.wait("vector", "s", sv3)
        vv4 = sch.op("vector", lambda e: e.tensor_tensor(stC[:, 4:5], stB[:, 6:7], stA[:, 10:11], ALU.subtract), inc="v")
        sch.wait("scalar", "v", vv4)
        sv4 = sch.op("scalar", lambda e: e.activation(stB[:, 10:11], stC[:, 4:5], AF.Exp, scale=0.5), inc="s")
        sch.wait("vector", "s", sv4)
        sch.op("vector", lambda e: e.tensor_tensor(stot[:, ch:ch + 1], stB[:, 10:11], sqs[:, gch:gch + 1], ALU.mult))
        vpad()
        vxs = sch.op("vector", lambda e: e.tensor_scalar(xs, xn_tm, stot[:, ch:ch + 1], None, ALU.mult), inc="v")
        return sxn, vxs

    def xn_transposes(i, ch, gch, sxn):
        for b in range(2):
            psb = ps_tr[b]
            mm_guard(psb)
            sch.wait("tensor", "s", sxn)
            for k in range(4):
                dc = b * 4 + k
                tv = sch.op("tensor", lambda e, dc=dc, k=k, psb=psb: e.transpose(
                    psb[:, k * 128:(k + 1) * 128],
                    xn_tm[:, dc * 128:(dc + 1) * 128], ident), inc="t")
            mm_done(psb, tv)
            ev_xn_tr[gch] = tv
            ev_guard("vector", psb)
            dco = b * 4
            hidst = xnf_hi.rearrange("p (c n) -> p c n", n=TA + 6)[:, dco:dco + 4, 3 + ch * 128:3 + ch * 128 + 128]
            lodst = xnf_lo.rearrange("p (c n) -> p c n", n=TA + 6)[:, dco:dco + 4, 3 + ch * 128:3 + ch * 128 + 128]
            ps3 = psb[:, 0:512].rearrange("p (c n) -> p c n", n=128)
            sch.op("vector", lambda e, d=hidst, s=ps3: e.tensor_copy(d, s))
            ve = sch.op("vector", lambda e, d=lodst, s=ps3, h=hidst: e.tensor_tensor(d, s, h, ALU.subtract), inc="v")
            ev_done(psb, "v", ve)
    def round_math(i, ch, gch):
        xs = xslot[gch % 2]
        sch.wait("vector", "t", ev_xi_tr.get(gch - 1, 0))
        xiv = None
        for qc in range(NSEG):
            seg = slice(qc * 512, (qc + 1) * 512)
            sch.op("vector", lambda e, s2=seg: e.tensor_scalar(u_b, xs[:, s2], -1.0, -128.5, ALU.mult, ALU.add))
            vpad()
            for b in range(NBITS - 1, -1, -1):
                tb = float(2 ** b)
                vc = sch.op("vector", lambda e, tb=tb: e.tensor_scalar(
                    c_b, u_b, -tb, tb, ALU.is_le, ALU.mult), inc="v")
                sch.wait("gpsimd", "v", vc)
                gw = sch.op("gpsimd", lambda e: e.tensor_tensor(u_b, u_b, c_b, ALU.add), inc="g")
                sch.wait("vector", "g", gw)
            xiv = sch.op("vector", lambda e, s2=seg: e.scalar_tensor_tensor(
                xi_tm[:, s2], u_b, 0.5, xs[:, s2], ALU.add, ALU.add), inc="v")
        ev_xfree[gch] = xiv
        return xiv

    def xi_tr_part(i, ch, gch, xiv):
        for b in range(2):
            psb = ps_tr[b]
            mm_guard(psb)
            sch.wait("tensor", "v", xiv)
            for k in range(4):
                dc = b * 4 + k
                tv = sch.op("tensor", lambda e, dc=dc, k=k, psb=psb: e.transpose(
                    psb[:, k * 128:(k + 1) * 128],
                    xi_tm[:, dc * 128:(dc + 1) * 128], ident), inc="t")
            mm_done(psb, tv)
            ev_xi_tr[gch] = tv
            ev_guard("vector", psb)
            dco = b * 4
            dst = xi_fm.rearrange("p (c n) -> p c n", n=128)[:, dco:dco + 4, :]
            ps3 = psb[:, 0:512].rearrange("p (c n) -> p c n", n=128)
            ve = sch.op("vector", lambda e, d=dst, s2=ps3: e.tensor_copy(d, s2), inc="v")
            ev_done(psb, "v", ve)
        sch.wait("sync", "v", ve)
        sch.op("sync", lambda e, g=gch: e.dma_start(
            out=xint_d[:, :, g * 128:(g + 1) * 128],
            in_=xi_fm.rearrange("p (c n) -> p c n", n=128)), inc="st", dma=True)

    def mm3(psum_ap, ps_handle, st_hi, st_lo, mv_hi, mv_lo, first, last, tp=None):
        sch.op("tensor", lambda e: e.matmul(psum_ap, st_hi, mv_hi, start=first, stop=False, tile_position=tp), inc="t")
        sch.op("tensor", lambda e: e.matmul(psum_ap, st_hi, mv_lo, start=False, stop=False, tile_position=tp), inc="t")
        return sch.op("tensor", lambda e: e.matmul(psum_ap, st_lo, mv_hi, start=False, stop=last, tile_position=tp), inc="t")

    SERIAL = os.environ.get("K_SERIAL") == "1"
    ev_sall_free = [0]
    ev_qproj_s = {}
    ev_kstrip_s = {}

    def emit_kproj(i):
        kb = kbuf[i % 2]
        NKC = TA + 3
        ve = 0
        for oc in range(NDC):
            psum = ps_pj[oc % 2]
            mm_guard(psum)
            for dc in range(NDC):
                tv = mm3(psum[:, 0:NKC], psum,
                         wn["wk_hi"][:, dc * D + oc * 128:dc * D + oc * 128 + 128],
                         wn["wk_lo"][:, dc * D + oc * 128:dc * D + oc * 128 + 128],
                         xnf_hi[:, dc * (TA + 6):dc * (TA + 6) + NKC],
                         xnf_lo[:, dc * (TA + 6):dc * (TA + 6) + NKC],
                         dc == 0, dc == NDC - 1)
            mm_done(psum, tv)
            ev_guard("scalar", psum)
            ve = sch.op("scalar", lambda e, o=oc, p=psum, kk=kb: e.copy(
                kk[:, o * (TA + 6):o * (TA + 6) + NKC], p[:, 0:NKC]), inc="s")
            ev_done(psum, "s", ve)
        if i >= 1:
            kprev = kbuf[(i - 1) % 2]
            src3 = kb.rearrange("p (c n) -> p c n", n=TA + 6)[:, :, 3:6]
            dst3 = kprev.rearrange("p (c n) -> p c n", n=TA + 6)[:, :, TA + 3:TA + 6]
            ev_kstrip_s[i - 1] = sch.op("scalar", lambda e, s2=src3, d=dst3: e.copy(d, s2), inc="s")
        return ve

    def emit_vhead(i):
        ve = 0
        for ob2 in range(2):
            psum = ps_pj[ob2]
            mm_guard(psum)
            for dc in range(NDC):
                tv = mm3(psum[:], psum,
                         xnf_hi[:, dc * (TA + 6):dc * (TA + 6) + 128],
                         xnf_lo[:, dc * (TA + 6):dc * (TA + 6) + 128],
                         wn["wv_hi"][:, dc * D + ob2 * 512:dc * D + ob2 * 512 + 512],
                         wn["wv_lo"][:, dc * D + ob2 * 512:dc * D + ob2 * 512 + 512],
                         dc == 0, dc == NDC - 1)
            mm_done(psum, tv)
            ev_guard("scalar", psum)
            ve = sch.op("scalar", lambda e, o=ob2, p=psum: e.copy(
                vhead[:, o * 512:(o + 1) * 512], p[:]), inc="s")
            ev_done(psum, "s", ve)
        return ("s", ve)

    def emit_scores(ti, ch):
        kprev = kbuf[ti % 2]
        for b0, nb in [(0, 3), (3, 3), (6, 2)]:
            mm_guard(ps_sc[0])
            mm_guard(ps_sc[1])
            for pi in range(b0, b0 + nb):
                sl = pi - b0
                qs0 = qbuf[0:64, pi * TA + ch * 128:pi * TA + ch * 128 + 128]
                qs1 = qbuf[64:128, pi * TA + ch * 128:pi * TA + ch * 128 + 128]
                ks0 = kprev[0:64, pi * (TA + 6) + ch * 128:pi * (TA + 6) + ch * 128 + KW]
                ks1 = kprev[64:128, pi * (TA + 6) + ch * 128:pi * (TA + 6) + ch * 128 + KW]
                sch.op("tensor", lambda e, q=qs0, k2=ks0, sl_=sl: e.matmul(
                    ps_sc[0][:, sl_ * KW:(sl_ + 1) * KW], q, k2, start=True, stop=True,
                    tile_position=(0, 0)), inc="t")
                tv = sch.op("tensor", lambda e, q=qs1, k2=ks1, sl_=sl: e.matmul(
                    ps_sc[1][:, sl_ * KW:(sl_ + 1) * KW], q, k2, start=True, stop=True,
                    tile_position=(64, 0)), inc="t")
            mm_done(ps_sc[0], tv)
            mm_done(ps_sc[1], tv)
            for par in range(2):
                psb = ps_sc[par]
                ev_guard("vector", psb)
                sch.wait("vector", "t", ev_sall_free[0])
                h0 = 2 * b0 + par
                dst = sall.rearrange("p (h n) -> p h n", n=KW)[:, h0:h0 + 2 * nb - 1:2, :]
                src = psb[:, 0:nb * KW].rearrange("p (s n) -> p s n", n=KW)
                m1b = m1t.unsqueeze(1).broadcast_to([128, nb, KW])
                ve = sch.op("vector", lambda e, d=dst, s2=src, m=m1b: e.scalar_tensor_tensor(
                    d, s2, 0.125, m, ALU.mult, ALU.add), inc="v")
                ev_done(psb, "v", ve)

    def emit_softmax(ti, ch):
        vm = sch.op("vector", lambda e: e.reduce_max(negm[:], sall, AX.X, negate=True), inc="v")
        sch.wait("scalar", "v", vm)
        se = sch.op("scalar", lambda e: e.activation(sall, sall, AF.Exp, bias=negm[:]), inc="s")
        sch.wait("vector", "s", se)
        sch.op("vector", lambda e: e.tensor_reduce(
            abd_a[:], sall.rearrange("p (h n) -> p h n", n=KW), AX.X, ALU.add))
        for h in range(H):
            ssl = sall[:, h * KW:(h + 1) * KW]
            csl = cbt[:, h * KW:(h + 1) * KW]
            sch.op("vector", lambda e, s2=ssl, c=csl, h=h: e.scalar_tensor_tensor(
                s2, s2, 1.0, c, ALU.mult, ALU.mult, accum_out=abd_b[:, h:h + 1]))
        vpad()
        sch.op("vector", lambda e: e.scalar_tensor_tensor(
            abd_d[:], abd_a[:], 1e-9, abd_b[:], ALU.mult, ALU.add))
        vpad()
        sch.op("vector", lambda e: e.reciprocal(abd_i[:], abd_d[:]))
        vpad()
        fw_evs = []
        for h in range(H):
            ssl = sall[:, h * KW:(h + 1) * KW]
            vf = sch.op("vector", lambda e, s2=ssl, h=h: e.tensor_scalar(
                s2, s2, abd_i[:, h:h + 1], None, ALU.mult), inc="v")
            fw_evs.append(vf)
        if _dbga:
            sch.wait("sync", "v", vf)
            dsv = sch.op("sync", lambda e, g=ti * CH_A + ch: e.dma_start(
                out=dbg_sall[g], in_=sall), inc="st", dma=True)
            sch.wait("tensor", "st", dsv)
        return fw_evs

    def emit_trout(ti, ch, fw_evs):
        gch = ti * CH_A + ch
        ev_qta_read = 0
        last_out = 0
        for tb in range(4):
            mm_guard(ps_tr[0])
            mm_guard(ps_tr[1])
            sch.wait("tensor", "v", fw_evs[tb * 4 + 3])
            for k in range(4):
                h = tb * 4 + k
                sch.op("tensor", lambda e, h=h, k=k: e.transpose(
                    ps_tr[0][:, k * 128:(k + 1) * 128],
                    sall[:, h * KW:h * KW + 128], ident), inc="t")
                tv = sch.op("tensor", lambda e, h=h, k=k: e.transpose(
                    ps_tr[1][0:6, k * 128:(k + 1) * 128],
                    sall[:, h * KW + 128:h * KW + KW], ident), inc="t")
            mm_done(ps_tr[0], tv)
            mm_done(ps_tr[1], tv)
            if tb == 3:
                ev_sall_free[0] = tv
            ev_guard("scalar", ps_tr[0])
            ev_guard("scalar", ps_tr[1])
            sch.wait("scalar", "t", ev_qta_read)
            sch.op("scalar", lambda e: e.copy(qta, ps_tr[0][:, 0:512]))
            ve = sch.op("scalar", lambda e: e.copy(qtb[0:6, :], ps_tr[1][0:6, 0:512]), inc="s")
            ev_done(ps_tr[0], "s", ve)
            ev_done(ps_tr[1], "s", ve)
            sch.wait("tensor", "s", ve)
            psb = ps_ou[tb // 2]
            if tb % 2 == 0:
                mm_guard(psb)
            for pp in range(2):
                p = tb * 2 + pp
                for hh in range(2):
                    h = 2 * p + hh
                    slot = h % 4
                    v1s = vbuf[:, ch * D + h * 64:ch * D + h * 64 + 64]
                    if ch < CH_A - 1:
                        v2s = vbuf[0:32, (ch + 1) * D + h * 64:(ch + 1) * D + h * 64 + 64]
                    else:
                        v2s = vhead[0:32, h * 64:h * 64 + 64]
                    q1 = qta[:, slot * 128:(slot + 1) * 128]
                    q2 = qtb[0:32, slot * 128 + 122:slot * 128 + 128]
                    po = psb[hh * 64:(hh + 1) * 64, (p % 4) * 128:(p % 4) * 128 + 128]
                    po2 = psb[hh * 64:(hh + 1) * 64, (p % 4) * 128 + 122:(p % 4) * 128 + 128]
                    sch.op("tensor", lambda e, v=v1s, q=q1, o=po, hh=hh: e.matmul(
                        o, v, q, start=True, stop=False, tile_position=(0, hh * 64)), inc="t")
                    tv = sch.op("tensor", lambda e, v=v2s, q=q2, o=po2, hh=hh: e.matmul(
                        o, v, q, start=False, stop=True, tile_position=(0, hh * 64)), inc="t")
            mm_done(psb, tv)
            ev_qta_read = tv
            last_out = tv
            if tb % 2 == 1:
                ev_guard("scalar", psb)
                sch.wait("scalar", "st", ev_attn_bw.get(gch, 0))
                k4 = tb // 2
                dst = attn_b.rearrange("p (c n) -> p c n", n=128)[:, 0:4, :]
                ve = sch.op("scalar",
                            lambda e, d=dst, s2=psb[:, 0:512].rearrange("p (c n) -> p c n", n=128):
                            e.copy(d, s2), inc="s")
                ev_done(psb, "s", ve)
                sch.wait("sync", "s", ve)
                sv = sch.op("sync", lambda e, g=gch, k4=k4: e.dma_start(
                    out=attn_d[:, 4 * k4:4 * k4 + 4, g * 128:(g + 1) * 128],
                    in_=attn_b.rearrange("p (c n) -> p c n", n=128)[:, 0:4, :]),
                    inc="st", dma=True)
                ev_attn_bw[gch] = sv
        return last_out

    def emit_qproj(i):
        ve = 0
        for oc in range(NDC):
            psum = ps_pj[oc % 2]
            mm_guard(psum)
            for dc in range(NDC):
                tv = mm3(psum[:, 0:TA], psum,
                         wn["wq_hi"][:, dc * D + oc * 128:dc * D + oc * 128 + 128],
                         wn["wq_lo"][:, dc * D + oc * 128:dc * D + oc * 128 + 128],
                         xnf_hi[:, dc * (TA + 6) + 3:dc * (TA + 6) + 3 + TA],
                         xnf_lo[:, dc * (TA + 6) + 3:dc * (TA + 6) + 3 + TA],
                         dc == 0, dc == NDC - 1)
            mm_done(psum, tv)
            ev_guard("scalar", psum)
            ve = sch.op("scalar", lambda e, o=oc, p=psum: e.copy(
                qbuf[:, o * TA:(o + 1) * TA], p[:, 0:TA]), inc="s")
            ev_done(psum, "s", ve)
        ev_qproj_s[i] = ve

    def emit_vproj_rest(i, ev_vbuf_free):
        vh_s, vh_v = ev_vhead[i]
        sch.wait("vector", vh_s, vh_v)
        sch.wait("vector", "t", ev_vbuf_free)
        ve = sch.op("vector", lambda e: e.tensor_copy(vbuf[:, 0:D], vhead[:, 0:D]), inc="v")
        for ob2 in range(2):
            psum = ps_pj[ob2]
            mm_guard(psum)
            for dc in range(NDC):
                tv = mm3(psum[:], psum,
                         xnf_hi[:, dc * (TA + 6) + 128:dc * (TA + 6) + 256],
                         xnf_lo[:, dc * (TA + 6) + 128:dc * (TA + 6) + 256],
                         wn["wv_hi"][:, dc * D + ob2 * 512:dc * D + ob2 * 512 + 512],
                         wn["wv_lo"][:, dc * D + ob2 * 512:dc * D + ob2 * 512 + 512],
                         dc == 0, dc == NDC - 1)
            mm_done(psum, tv)
            ev_guard("vector", psum)
            ve = sch.op("vector", lambda e, o=ob2, p=psum: e.tensor_copy(
                vbuf[:, D + o * 512:D + o * 512 + 512], p[:, 0:512]), inc="v")
            ev_done(psum, "v", ve)
        return ve

    for i in range(NT_A + 1):
        if SERIAL:
            sch.barrier()
        ti = i - 1
        if i < NT_A:
            for ch in range(CH_A):
                gch = i * CH_A + ch
                sch.wait("sync", "v", ev_xfree.get(gch - 2, 0))
                ev_xload[gch] = sch.op("sync", lambda e, g=gch, sl=xslot[gch % 2]: e.dma_start(
                    out=sl, in_=xio.rearrange("(c p) d -> p c d", p=128)[:, g, :]),
                    inc="ld", dma=True)

        if i < NT_A:
            sch.wait("vector", "t", ev_proj_read.get(i - 1, 0))
            x3h = xnf_hi.rearrange("p (c n) -> p c n", n=TA + 6)[:, :, 0:3]
            x3l = xnf_lo.rearrange("p (c n) -> p c n", n=TA + 6)[:, :, 0:3]
            s3h = strip_hi.rearrange("p (c n) -> p c n", n=3)
            s3l = strip_lo.rearrange("p (c n) -> p c n", n=3)
            sch.op("vector", lambda e: e.tensor_copy(x3h, s3h))
            sch.op("vector", lambda e: e.tensor_copy(x3l, s3l))
            for ch in range(CH_A):
                gch = i * CH_A + ch
                sxn, _ = ln_core(i, ch, gch)
                xn_transposes(i, ch, gch, sxn)
            ev_xnf[i] = sch.cnt["v"]
            sph = xnf_hi.rearrange("p (c n) -> p c n", n=TA + 6)[:, :, TA:TA + 3]
            spl = xnf_lo.rearrange("p (c n) -> p c n", n=TA + 6)[:, :, TA:TA + 3]
            sch.op("vector", lambda e: e.tensor_copy(s3h, sph))
            sch.op("vector", lambda e: e.tensor_copy(s3l, spl))

        if i == 0:
            sch.wait("tensor", "v", ev_xnf[0])
            emit_kproj(0)
            ev_vhead[0] = emit_vhead(0)
            emit_qproj(0)
            emit_vproj_rest(0, 0)
            xv = round_math(0, 0, 0)
            xi_tr_part(0, 0, 0, xv)
            xv = round_math(0, 1, 1)
            xi_tr_part(0, 1, 1, xv)
            ev_proj_read[0] = sch.cnt["t"]
            continue

        if SERIAL:
            sch.barrier()
        sch.wait("tensor", "v", cb_done)
        sch.wait("tensor", "s", ev_qproj_s.get(ti, 0))
        if _dbga:
            sch.wait("tensor", "s", ev_kstrip_s.get(ti, 0))
            kprev = kbuf[ti % 2]
            sch.op("sync", lambda e, t=ti: e.dma_start(out=dbg_q[t], in_=qbuf), inc="st", dma=True)
            sch.op("sync", lambda e, t=ti, kp=kprev: e.dma_start(out=dbg_k[t], in_=kp), inc="st", dma=True)
            sch.op("sync", lambda e, t=ti: e.dma_start(out=dbg_v[t], in_=vbuf), inc="st", dma=True)
            dbv = sch.op("sync", lambda e, t=ti: e.dma_start(out=dbg_vh[t], in_=vhead), inc="st", dma=True)
            sch.wait("tensor", "st", dbv)

        emit_scores(ti, 0)
        fw0 = emit_softmax(ti, 0)
        if i < NT_A:
            sch.wait("tensor", "v", ev_xnf[i])
            emit_kproj(i)
            ev_vhead[i] = emit_vhead(i)
        else:
            kprev = kbuf[(i - 1) % 2]
            dst3 = kprev.rearrange("p (c n) -> p c n", n=TA + 6)[:, :, TA + 3:TA + 6]
            ev_kstrip_s[i - 1] = sch.op("scalar", lambda e, d=dst3: e.memzero(d), inc="s")
            vz = sch.op("vector", lambda e: e.memset(vhead, 0.0), inc="v")
            sch.wait("tensor", "v", vz)
            for ob2 in range(2):
                psum = ps_pj[ob2]
                mm_guard(psum)
                for dc in range(NDC):
                    tv = mm3(psum[0:3, :], psum,
                             strip_hi[:, dc * 3:dc * 3 + 3],
                             strip_lo[:, dc * 3:dc * 3 + 3],
                             wn["wv_hi"][:, dc * D + ob2 * 512:dc * D + ob2 * 512 + 512],
                             wn["wv_lo"][:, dc * D + ob2 * 512:dc * D + ob2 * 512 + 512],
                             dc == 0, dc == NDC - 1)
                mm_done(psum, tv)
                ev_guard("vector", psum)
                vz = sch.op("vector", lambda e, o=ob2, p=psum: e.tensor_copy(
                    vhead[0:3, o * 512:(o + 1) * 512], p[0:3, :]), inc="v")
                ev_done(psum, "v", vz)
            ev_vhead[i] = ("v", vz)
        vh_sem, vh_val = ev_vhead[i]
        sch.wait("tensor", vh_sem, vh_val)
        out0 = emit_trout(ti, 0, fw0)
        if i < NT_A:
            xiv0 = round_math(i, 0, i * CH_A)
            xi_tr_part(i, 0, i * CH_A, xiv0)
        sch.wait("tensor", "s", ev_kstrip_s.get(ti, 0))
        emit_scores(ti, 1)
        fw1 = emit_softmax(ti, 1)
        if i < NT_A:
            emit_qproj(i)
            xiv1 = round_math(i, 1, i * CH_A + 1)
            xi_tr_part(i, 1, i * CH_A + 1, xiv1)
        out1 = emit_trout(ti, 1, fw1)
        if i < NT_A:
            emit_vproj_rest(i, out0)
        ev_proj_read[i] = sch.cnt["t"]

    finA = {s: sch.cnt[s] for s in ("t", "v", "s", "g", "st")}

    for eng in ("sync", "tensor", "vector", "scalar", "gpsimd"):
        for s in ("t", "v", "s", "g"):
            if (eng, s) not in (("tensor", "t"), ("vector", "v"), ("scalar", "s"), ("gpsimd", "g")):
                sch.wait(eng, s, finA[s])
    sch.wait("sync", "st", finA["st"])

    stageB = xb2[:, 0:2 * D]
    last_split = 0
    for piece in range(4):
        dc0 = piece * 2
        sch.wait("sync", "v", last_split)
        lv = sch.op("sync",
                    lambda e, dc0=dc0: e.dma_start(
                        out=stageB,
                        in_=woT.rearrange("(c p) o -> p c o", p=128)[:, dc0:dc0 + 2, :]),
                    inc="ld", dma=True)
        sch.wait("vector", "ld", lv)
        for j in range(2):
            dc = dc0 + j
            src = stageB[:, j * D:(j + 1) * D]
            hi = wn["wq_hi"][:, dc * D:(dc + 1) * D]
            lo = wn["wq_lo"][:, dc * D:(dc + 1) * D]
            sch.op("vector", lambda e, s=src, h=hi: e.tensor_copy(h, s))
            last_split = sch.op("vector",
                                lambda e, s=src, h=hi, l=lo: e.tensor_tensor(l, s, h, ALU.subtract),
                                inc="v")
    lb1 = sch.op("sync", lambda e: e.dma_start(out=bo_t, in_=bo_rep), inc="ld", dma=True)
    lb2 = sch.op("sync", lambda e: e.dma_start(out=bf_t, in_=bf_rep), inc="ld", dma=True)
    sch.wait("vector", "ld", lb2)
    sch.op("vector", lambda e: e.reciprocal(rq_b[:], sqs[:]))
    vpad()
    gsv = sch.op("vector", lambda e: e.tensor_scalar(gs_b[:], rq_b[:], ws_b[:, 0:1], None, ALU.mult), inc="v")
    wo_ready = last_split
    sch.wait("sync", "v", wo_ready)

    ev_bld = {}
    ev_bfree = {}
    ev_outdma = {}
    ev_mm_read = 0

    def emit_b_loads(j):
        t0 = j * TB
        par = j % 2
        sch.wait("sync", "st", ev_outdma.get(j - 1, 0))
        sch.wait("sync", "v", ev_bfree.get(j - 1, 0))
        l1 = sch.op("sync", lambda e, t0=t0: e.dma_start(
            out=ab_buf.rearrange("p (c n) -> p c n", n=TB),
            in_=attn_d[:, :, t0:t0 + TB]), inc="ld", dma=True)
        l2 = sch.op("sync", lambda e, t0=t0, p=par: e.dma_start(
            out=xib[p].rearrange("p (c n) -> p c n", n=TB),
            in_=xint_d[:, :, t0:t0 + TB]), inc="ld", dma=True)
        l3 = sch.op("sync", lambda e, t0=t0: e.dma_start(
            out=xb2.rearrange("p (c n) -> p c n", n=D),
            in_=xio.rearrange("(c p) d -> p c d", p=128)[:, t0 // 128:t0 // 128 + CH_B, :]),
            inc="ld", dma=True)
        ev_bld[j] = (l1, l2, l3)

    emit_b_loads(0)
    for j in range(1, NT_B + 1):
        jj = j - 1
        par = jj % 2
        t0 = jj * TB
        l1, l2, l3 = ev_bld[jj]
        sch.wait("vector", "ld", l1)
        sch.wait("vector", "t", ev_mm_read)
        for dc in range(NDC):
            src = ab_buf[:, dc * TB:(dc + 1) * TB]
            hi = ah_b[:, dc * TB:(dc + 1) * TB]
            lo = al_b[:, dc * TB:(dc + 1) * TB]
            sch.op("vector", lambda e, s=src, h=hi: e.tensor_copy(h, s))
            vsp = sch.op("vector", lambda e, s=src, h=hi, l=lo: e.tensor_tensor(l, s, h, ALU.subtract), inc="v")
        s_tm = ab_buf
        sch.wait("tensor", "v", vsp)
        for tc in range(CH_B):
            for ob2 in range(2):
                psum = ps_pj[(tc * 2 + ob2) % 2]
                mm_guard(psum)
                for dc in range(NDC):
                    tv = mm3(psum[:], psum,
                             ah_b[:, dc * TB + tc * 128:dc * TB + tc * 128 + 128],
                             al_b[:, dc * TB + tc * 128:dc * TB + tc * 128 + 128],
                             wn["wq_hi"][:, dc * D + ob2 * 512:dc * D + ob2 * 512 + 512],
                             wn["wq_lo"][:, dc * D + ob2 * 512:dc * D + ob2 * 512 + 512],
                             dc == 0, dc == NDC - 1)
                mm_done(psum, tv)
                ev_guard("vector", psum)
                ve = sch.op("vector", lambda e, t=tc, o=ob2, p=psum: e.scalar_tensor_tensor(
                    s_tm[:, t * D + o * 512:t * D + o * 512 + 512], p[:], 1.0,
                    bo_t[:, o * 512:(o + 1) * 512], ALU.mult, ALU.add), inc="v")
                ev_done(psum, "v", ve)
        sch.wait("tensor", "ld", l2)
        sch.wait("tensor", "v", gsv)
        for tc in range(CH_B):
            gch = jj * CH_B + tc
            for ob2 in range(2):
                psum = ps_ou[ob2]
                mm_guard(psum)
                for dc in range(NDC):
                    tv = sch.op("tensor", lambda e, dc=dc, p=psum, t=tc, o=ob2, pr=par: e.matmul(
                        p[:], xib[pr][:, dc * TB + t * 128:dc * TB + t * 128 + 128],
                        wi8_sb[:, dc * D + o * 512:dc * D + o * 512 + 512],
                        start=(dc == 0), stop=(dc == NDC - 1)), inc="t")
                mm_done(psum, tv)
                ev_guard("vector", psum)
                vg = sch.op("vector", lambda e, t=tc, o=ob2, p=psum, g=gch: e.scalar_tensor_tensor(
                    g_tm[:, t * D + o * 512:t * D + o * 512 + 512], p[:], gs_b[:, g:g + 1],
                    bf_t[:, o * 512:(o + 1) * 512], ALU.mult, ALU.add), inc="v")
                ev_done(psum, "v", vg)
            sch.wait("scalar", "v", vg)
            sg = sch.op("scalar", lambda e, t=tc: e.activation(
                g_tm[:, t * D:(t + 1) * D], g_tm[:, t * D:(t + 1) * D], AF.Silu), inc="s")
            sch.wait("vector", "s", sg)
            sch.wait("vector", "ld", l3)
            sch.op("vector", lambda e, t=tc: e.tensor_tensor(
                g_tm[:, t * D:(t + 1) * D], g_tm[:, t * D:(t + 1) * D],
                s_tm[:, t * D:(t + 1) * D], ALU.mult))
            sch.op("vector", lambda e, t=tc: e.tensor_tensor(
                xb2[:, t * D:(t + 1) * D], xb2[:, t * D:(t + 1) * D],
                g_tm[:, t * D:(t + 1) * D], ALU.add))
        ev_mm_read = sch.cnt["t"]
        vfree = sch.op("vector", lambda e: e.engine_nop(), inc="v")
        ev_bfree[jj] = vfree
        sch.wait("sync", "v", vfree)
        ev_outdma[jj] = sch.op("sync", lambda e, t0=t0: e.dma_start(
            out=out_d.rearrange("(c p) d -> p c d", p=128)[:, t0 // 128:t0 // 128 + CH_B, :],
            in_=xb2.rearrange("p (c n) -> p c n", n=D)), inc="st", dma=True)
        if j < NT_B:
            emit_b_loads(j)

    sch.wait("sync", "st", sch.cnt["st"])

    with nc.Block() as block:
        @block.sync
        def _(sync):
            for fn in sch.prog["sync"]:
                fn(sync)

        @block.tensor
        def _(tensor):
            for fn in sch.prog["tensor"]:
                fn(tensor)

        @block.vector
        def _(vector):
            for fn in sch.prog["vector"]:
                fn(vector)

        @block.scalar
        def _(scalar):
            for fn in sch.prog["scalar"]:
                fn(scalar)

        @block.gpsimd
        def _(gpsimd):
            for fn in sch.prog["gpsimd"]:
                fn(gpsimd)

    return nc


def _host_consts():
    f32 = np.float32
    ident = np.eye(128, dtype=f32)
    ones1 = np.ones((1, 128), dtype=f32)
    t_idx = np.arange(128)[:, None]
    k_idx = np.arange(KW)[None, :]
    band = (k_idx - t_idx >= 0) & (k_idx - t_idx <= 6)
    m1 = np.where(band, 0.0, -np.inf).astype(f32)
    dmask = np.zeros((128, W7 * KW), dtype=f32)
    for w in range(W7):
        dmask[:, w * KW:(w + 1) * KW] = (k_idx - t_idx == w).astype(f32)
    return ident, ones1, m1, dmask


def kernel(L0_attn_x, L0_attn_wq, L0_attn_wk, L0_attn_wv, L0_attn_wo,
           L0_attn_bo, L0_attn_cw1, L0_attn_cb1, L0_attn_cw2, L0_attn_cb2,
           L0_attn_cw3, L0_attn_cb3, L0_ffn_w_int8, L0_ffn_weight_scale,
           L0_ffn_bias):
    x = np.ascontiguousarray(np.asarray(L0_attn_x, dtype=np.float32))
    B, S, d = x.shape
    assert d == D
    f32 = np.float32

    wqT = np.ascontiguousarray(np.asarray(L0_attn_wq, f32).T)
    wkT = np.ascontiguousarray(np.asarray(L0_attn_wk, f32).T)
    wvT = np.ascontiguousarray(np.asarray(L0_attn_wv, f32).T)
    woT = np.ascontiguousarray(np.asarray(L0_attn_wo, f32).T)
    wi8T = np.ascontiguousarray(np.asarray(L0_ffn_w_int8, np.int32).T)
    bo_rep = np.ascontiguousarray(np.broadcast_to(np.asarray(L0_attn_bo, f32), (128, D)))
    bf_rep = np.ascontiguousarray(np.broadcast_to(np.asarray(L0_ffn_bias, f32), (128, D)))
    ws = np.asarray(L0_ffn_weight_scale, f32).reshape(1, 1)

    pos = np.linspace(0.0, 1.0, W7, dtype=f32)
    pos_t = np.ascontiguousarray(np.broadcast_to(pos[None, :, None], (H, W7, 32)).reshape(H, 224))
    cw1 = np.asarray(L0_attn_cw1, f32)
    cb1 = np.asarray(L0_attn_cb1, f32)
    cw2 = np.asarray(L0_attn_cw2, f32)
    cb2 = np.asarray(L0_attn_cb2, f32)
    cw3 = np.asarray(L0_attn_cw3, f32)
    cb3 = np.asarray(L0_attn_cb3, f32)
    cw1r = np.ascontiguousarray(np.broadcast_to(cw1[:, None, :, 0], (H, W7, 32)).reshape(H, 224))
    cb1r = np.ascontiguousarray(np.broadcast_to(cb1[:, None, :], (H, W7, 32)).reshape(H, 224))
    cw2r = np.ascontiguousarray(cw2.reshape(H, 1024))
    cb2r = np.ascontiguousarray(np.broadcast_to(cb2[:, None, :], (H, W7, 32)).reshape(H, 224))
    cw3r = np.ascontiguousarray(np.broadcast_to(cw3, (H, W7, 32)).reshape(H, 224))
    cb3c = np.ascontiguousarray(cb3)

    ident, ones1, m1, dmask = _host_consts()

    nc = build_program(S)
    shared = dict(wqT=wqT, wkT=wkT, wvT=wvT, woT=woT, wi8T=wi8T,
                  bo_rep=bo_rep, bf_rep=bf_rep, ws=ws,
                  pos_t=pos_t, cw1r=cw1r, cb1r=cb1r, cw2r=cw2r, cb2r=cb2r,
                  cw3r=cw3r, cb3c=cb3c, ident=ident, ones1=ones1,
                  m1=m1, dmask=dmask)
    in_maps = [dict(shared, x=np.ascontiguousarray(x[b])) for b in range(B)]
    res = run_bass_kernel_spmd(nc, in_maps, core_ids=list(range(B)))
    if os.environ.get("K_DEBUG_SCRATCH") == "1":
        keys = ["attn_scr", "xint_scr", "conn_scr"]
        if os.environ.get("K_DEBUG_ATTN") == "1":
            keys += ["dbg_q", "dbg_k", "dbg_v", "dbg_vh", "dbg_sall", "dbg_k2"]
        for b in range(B):
            for k in keys:
                np.save(f"/tmp/k_dbg_{k}_{b}.npy", res.results[b][k])
    out = np.stack([r["out"] for r in res.results], axis=0)
    return out.astype(np.float32)


# version 07
# speedup vs baseline: 1.4565x; 1.0737x over previous
import os
import numpy as np
import ml_dtypes
import concourse.bass as bass
import concourse.mybir as mybir
from concourse.bass_utils import run_bass_kernel_spmd

F32 = mybir.dt.float32
BF16 = mybir.dt.bfloat16
I32 = mybir.dt.int32
AF = mybir.ActivationFunctionType
ALU = mybir.AluOpType
AX = mybir.AxisListType

D = 1024
H = 16
NDC = D // 128
TA = 256
TB = 512
W7 = 7
KW = 134


class Sched:
    ENGS = ("sync", "tensor", "vector", "scalar", "gpsimd")

    def __init__(self, nc):
        self.nc = nc
        self.prog = {e: [] for e in self.ENGS}
        self.sems = {}
        self.cnt = {}

    def sem(self, name):
        self.sems[name] = self.nc.alloc_semaphore(name)
        self.cnt[name] = 0

    def op(self, eng, fn, inc=None, dma=False):
        incs = []
        ret = None
        if inc is not None:
            k = 16 if dma else 1
            self.cnt[inc] += k
            ret = self.cnt[inc]
            incs.append((self.sems[inc], k))

        def run(e, fn=fn, incs=incs):
            inst = fn(e)
            for h, k in incs:
                inst.then_inc(h, k)
        self.prog[eng].append(run)
        return ret

    def wait(self, eng, sem, val):
        if val is None or val <= 0:
            return
        h = self.sems[sem]
        self.prog[eng].append(lambda e, h=h, v=val: e.wait_ge(h, v))

    def barrier(self):
        snap = {s: self.cnt[s] for s in ("t", "v", "s", "g", "ld", "st")}
        own = {"tensor": "t", "vector": "v", "scalar": "s", "gpsimd": "g"}
        for eng in self.ENGS:
            for s, v in snap.items():
                if own.get(eng) != s:
                    self.wait(eng, s, v)


def build_program(S):
    assert S % TB == 0
    NT_A = S // TA
    NT_B = S // TB
    CH_A = TA // 128
    CH_B = TB // 128
    NCHUNK = S // 128

    nc = bass.Bass("TRN2", target_bir_lowering=False)

    xio = nc.dram_tensor("x", [S, D], F32, kind="ExternalInput").ap()
    wqT = nc.dram_tensor("wqT", [D, D], F32, kind="ExternalInput").ap()
    wkT = nc.dram_tensor("wkT", [D, D], F32, kind="ExternalInput").ap()
    wvT = nc.dram_tensor("wvT", [D, D], F32, kind="ExternalInput").ap()
    woT = nc.dram_tensor("woT", [D, D], F32, kind="ExternalInput").ap()
    wi8T = nc.dram_tensor("wi8T", [D, D], I32, kind="ExternalInput").ap()
    bo_rep = nc.dram_tensor("bo_rep", [128, D], F32, kind="ExternalInput").ap()
    bf_rep = nc.dram_tensor("bf_rep", [128, D], F32, kind="ExternalInput").ap()
    ws_in = nc.dram_tensor("ws", [1, 1], F32, kind="ExternalInput").ap()
    pos_c = nc.dram_tensor("pos_t", [H, 224], F32, kind="ExternalInput").ap()
    cw1_c = nc.dram_tensor("cw1r", [H, 224], F32, kind="ExternalInput").ap()
    cb1_c = nc.dram_tensor("cb1r", [H, 224], F32, kind="ExternalInput").ap()
    cw2_c = nc.dram_tensor("cw2r", [H, 1024], F32, kind="ExternalInput").ap()
    cb2_c = nc.dram_tensor("cb2r", [H, 224], F32, kind="ExternalInput").ap()
    cw3_c = nc.dram_tensor("cw3r", [H, 224], F32, kind="ExternalInput").ap()
    cb3_c = nc.dram_tensor("cb3c", [H, 1], F32, kind="ExternalInput").ap()
    ident_c = nc.dram_tensor("ident", [128, 128], F32, kind="ExternalInput").ap()
    ones1_c = nc.dram_tensor("ones1", [1, 128], F32, kind="ExternalInput").ap()
    m1_c = nc.dram_tensor("m1", [128, KW], F32, kind="ExternalInput").ap()
    dmask_c = nc.dram_tensor("dmask", [128, W7 * KW], F32, kind="ExternalInput").ap()

    out_d = nc.dram_tensor("out", [S, D], F32, kind="ExternalOutput").ap()
    _dbga = os.environ.get("K_DEBUG_ATTN") == "1"
    if _dbga:
        NT_A_ = S // TA
        dbg_q = nc.dram_tensor("dbg_q", [NT_A_, 128, NDC * TA], F32, kind="ExternalOutput").ap()
        dbg_k = nc.dram_tensor("dbg_k", [NT_A_, 128, NDC * (TA + 6)], F32, kind="ExternalOutput").ap()
        dbg_v = nc.dram_tensor("dbg_v", [NT_A_, 128, CH_A * D], F32, kind="ExternalOutput").ap()
        dbg_vh = nc.dram_tensor("dbg_vh", [NT_A_, 128, D], F32, kind="ExternalOutput").ap()
        dbg_k2 = nc.dram_tensor("dbg_k2", [NT_A_, 128, NDC * (TA + 6)], F32, kind="ExternalOutput").ap()
        dbg_sall = nc.dram_tensor("dbg_sall", [S // 128, 128, H * KW], F32, kind="ExternalOutput").ap()
    _dbg = os.environ.get("K_DEBUG_SCRATCH") == "1"
    _scr = "ExternalOutput" if _dbg else "Internal"
    attn_d = nc.dram_tensor("attn_scr", [128, NDC, S], F32, kind=_scr).ap()
    xint_d = nc.dram_tensor("xint_scr", [128, NDC, S], BF16, kind=_scr).ap()
    conn_d = nc.dram_tensor("conn_scr", [H, W7], F32, kind=_scr).ap()

    sbt = lambda name, shape, dt=F32: nc.alloc_sbuf_tensor(name, shape, dt)

    wn = {}
    for n in ("wq", "wk", "wv"):
        wn[n + "_hi"] = sbt(f"{n}_hi", [128, NDC * D], BF16)
        wn[n + "_lo"] = sbt(f"{n}_lo", [128, NDC * D], BF16)
    wi8_sb = sbt("wi8", [128, NDC * D], BF16)

    offs = {}
    cur = [0]

    def asl_mark(name, n, align=8):
        o = (cur[0] + align - 1) // align * align
        offs[name] = (o, n)
        cur[0] = o + n
        return o

    for nm, n in [("xs0", D), ("xs1", D), ("xn_tm", D),
                  ("qbuf", NDC * TA), ("kb0", NDC * (TA + 6)), ("kb1", NDC * (TA + 6)),
                  ("vbuf", CH_A * D), ("vhead", D), ("sall", H * KW),
                  ("qta", 512), ("qtb", 512), ("attn_b", 512),
                  ("u_b", 512), ("c_b", 512), ("xi_tm", D), ("cbt", H * KW),
                  ("m1t", KW), ("ident", 128)]:
        asl_mark(nm, n)
    A_END = cur[0]
    B_layout = [("ab", CH_B * D), ("xb2", CH_B * D), ("g_tm", CH_B * D),
                ("bo_t", D), ("bf_t", D)]
    bo = 0
    boffs = {}
    for nm, n in B_layout:
        boffs[nm] = (bo, n)
        bo += n
    AR_F32 = max(A_END, bo) + 16
    ar = sbt("arena_f32", [128, AR_F32], F32)

    def A(nm):
        o, n = offs[nm]
        return ar[:, o:o + n]

    def Bs(nm):
        o, n = boffs[nm]
        return ar[:, o:o + n]

    xslot = [A("xs0"), A("xs1")]
    xn_tm = A("xn_tm")
    qbuf = A("qbuf")
    kbuf = [A("kb0"), A("kb1")]
    vbuf = A("vbuf")
    vhead = A("vhead")
    sall = A("sall")
    qta = A("qta")
    qtb = A("qtb")
    attn_b = A("attn_b")
    u_b = A("u_b")
    c_b = A("c_b")
    xi_tm = A("xi_tm")
    cbt = A("cbt")
    m1t = A("m1t")
    ident = A("ident")
    dmk = ar[:, offs["qbuf"][0]:offs["qbuf"][0] + W7 * KW]
    stage = ar[:, offs["kb1"][0]:offs["kb1"][0] + 2 * D]
    cp_base = offs["kb0"][0]
    vb_base = offs["vbuf"][0]

    ab_buf = Bs("ab")
    xb2 = Bs("xb2")
    g_tm = Bs("g_tm")
    bo_t = Bs("bo_t")
    bf_t = Bs("bf_t")

    arb = sbt("arena_b16", [128, 2 * NDC * (TA + 6) + D + NDC * 128 + 32], BF16)
    ob = 0
    xnf_hi = arb[:, ob:ob + NDC * (TA + 6)]; ob += NDC * (TA + 6)
    xnf_lo = arb[:, ob:ob + NDC * (TA + 6)]; ob += NDC * (TA + 6)
    xi_fm = arb[:, ob:ob + NDC * 128]; ob += NDC * 128

    ah_b = wn["wk_hi"][:, 0:NDC * TB]
    al_b = wn["wk_lo"][:, 0:NDC * TB]
    xib = [wn["wv_hi"][:, 0:NDC * TB], wn["wv_lo"][:, 0:NDC * TB]]

    padA = sbt("padA", [128, 8], F32)
    padB = sbt("padB", [128, 8], F32)
    stA = sbt("stA", [128, 16], F32)
    stB = sbt("stB", [128, 16], F32)
    stC = sbt("stC", [128, 16], F32)
    amax = sbt("amax", [128, CH_A], F32)
    sqs = sbt("sqs", [128, NCHUNK], F32)
    stot = sbt("stot", [128, CH_A], F32)
    negm = sbt("negm", [128, 1], F32)
    abd_a = sbt("abd_a", [128, H], F32)
    abd_b = sbt("abd_b", [128, H], F32)
    abd_d = sbt("abd_d", [128, H], F32)
    abd_i = sbt("abd_i", [128, H], F32)
    gs_b = sbt("gs_b", [128, NCHUNK], F32)
    rq_b = sbt("rq_b", [128, NCHUNK], F32)
    ws_b = sbt("ws_b", [128, 1], F32)
    strip_hi = sbt("strip_hi", [128, NDC * 3], BF16)
    strip_lo = sbt("strip_lo", [128, NDC * 3], BF16)
    ident_b = sbt("ident_b", [128, 128], BF16)
    ones1 = sbt("ones1t", [1, 128], F32)
    cfl = sbt("cflat", [1, H * W7], F32)

    ps_tr = [nc.alloc_psum_tensor(f"ps_tr{i}", [128, 512], F32) for i in range(2)]
    ps_pj = [nc.alloc_psum_tensor(f"ps_pj{i}", [128, 512], F32) for i in range(2)]
    ps_sc = [nc.alloc_psum_tensor(f"ps_sc{i}", [128, 3 * KW], F32) for i in range(2)]
    ps_ou = [nc.alloc_psum_tensor(f"ps_ou{i}", [128, 512], F32) for i in range(2)]

    sch = Sched(nc)
    for s in ("ld", "st", "t", "v", "s", "g"):
        sch.sem(s)

    bank_ev = {}
    bank_mm = {}

    def key(ps):
        return id(ps)

    def mm_guard(ps):
        ev = bank_ev.get(key(ps))
        if ev:
            sch.wait("tensor", ev[0], ev[1])

    def mm_done(ps, tval):
        bank_mm[key(ps)] = tval

    def ev_guard(eng, ps):
        sch.wait(eng, "t", bank_mm.get(key(ps), 0))

    def ev_done(ps, sem, val):
        bank_ev[key(ps)] = (sem, val)

    def vpad():
        sch.op("vector", lambda e: e.tensor_copy(padB[:], padA[:]))

    for dst, src in [(ident, ident_c), (ones1[:], ones1_c), (m1t, m1_c),
                     (dmk, dmask_c)]:
        sch.op("sync", lambda e, d=dst, s=src: e.dma_start(out=d, in_=s),
               inc="ld", dma=True)
    cp = {}
    cpo = 0
    for nm, src, n in [("pos", pos_c, 224), ("cw1", cw1_c, 224), ("cb1", cb1_c, 224),
                       ("cw2", cw2_c, 1024), ("cb2", cb2_c, 224)]:
        cp[nm] = ar[0:H, cp_base + cpo:cp_base + cpo + n]
        sch.op("sync", lambda e, d=cp[nm], s=src: e.dma_start(out=d, in_=s),
               inc="ld", dma=True)
        cpo += n
    cpo = 0
    for nm, src, n in [("cw3", cw3_c, 224), ("cb3", cb3_c, 1)]:
        cp[nm] = ar[0:H, vb_base + cpo:vb_base + cpo + n]
        sch.op("sync", lambda e, d=cp[nm], s=src: e.dma_start(out=d, in_=s),
               inc="ld", dma=True)
        cpo += n
    cp["wsv"] = ar[0:1, vb_base + 256:vb_base + 257]
    sch.op("sync", lambda e: e.dma_start(out=cp["wsv"], in_=ws_in), inc="ld", dma=True)
    ld_consts = sch.cnt["ld"]

    last_split = 0
    for wname, wsrc in (("wq", wqT), ("wk", wkT), ("wv", wvT)):
        for piece in range(4):
            dc0 = piece * 2
            sch.wait("sync", "v", last_split)
            lv = sch.op("sync",
                        lambda e, w=wsrc, dc0=dc0: e.dma_start(
                            out=stage,
                            in_=w.rearrange("(c p) o -> p c o", p=128)[:, dc0:dc0 + 2, :]),
                        inc="ld", dma=True)
            sch.wait("vector", "ld", lv)
            for j in range(2):
                dc = dc0 + j
                src = stage[:, j * D:(j + 1) * D]
                hi = wn[wname + "_hi"][:, dc * D:(dc + 1) * D]
                lo = wn[wname + "_lo"][:, dc * D:(dc + 1) * D]
                sch.op("vector", lambda e, s=src, h=hi: e.tensor_copy(h, s))
                last_split = sch.op("vector",
                                    lambda e, s=src, h=hi, l=lo: e.tensor_tensor(l, s, h, ALU.subtract),
                                    inc="v")
    for piece in range(4):
        dc0 = piece * 2
        sch.wait("gpsimd", "v", last_split)
        lv = sch.op("gpsimd",
                    lambda e, dc0=dc0: e.dma_start(
                        out=stage,
                        in_=wi8T.rearrange("(c p) o -> p c o", p=128)[:, dc0:dc0 + 2, :]),
                    inc="ld", dma=True)
        sch.wait("vector", "ld", lv)
        for j in range(2):
            dc = dc0 + j
            src = stage[:, j * D:(j + 1) * D]
            dst = wi8_sb[:, dc * D:(dc + 1) * D]
            last_split = sch.op("vector", lambda e, s=src, d=dst: e.tensor_copy(d, s),
                                inc="v")

    sch.wait("vector", "ld", ld_consts)
    sch.op("vector", lambda e: e.memset(padA[:], 0.0))
    sch.op("vector", lambda e: e.tensor_copy(ident_b[:], ident))
    sch.op("vector", lambda e: e.memset(qtb, 0.0))
    sch.op("vector", lambda e: e.memset(strip_hi[:], 0.0))
    sch.op("vector", lambda e: e.memset(strip_lo[:], 0.0))
    init_v = sch.op("vector", lambda e: e.memset(cbt, 0.0), inc="v")

    h1 = sall[0:H, 0:224]
    h2 = sall[0:H, 256:480]
    tm3 = sall[0:H, 512:736]
    cwt = sall[0:H, 768:775]
    csml = sall[0:H, 800:808]

    sch.op("vector", lambda e: e.tensor_tensor(h1, cp["pos"], cp["cw1"], ALU.mult))
    v1 = sch.op("vector", lambda e: e.tensor_tensor(h1, h1, cp["cb1"], ALU.add), inc="v")
    sch.wait("scalar", "v", v1)
    s1 = sch.op("scalar", lambda e: e.activation(h1, h1, AF.Gelu), inc="s")
    sch.wait("vector", "s", s1)
    sch.op("vector", lambda e: e.memset(h2, 0.0))
    for i in range(32):
        h1_i = h1.rearrange("h (w i) -> h w i", i=32)[:, :, i:i + 1].broadcast_to([H, W7, 32])
        cw2_b = cp["cw2"].rearrange("h (j i) -> h i j", i=32)[:, i:i + 1, :].broadcast_to([H, W7, 32])
        t3 = tm3.rearrange("h (w j) -> h w j", j=32)
        h2_3 = h2.rearrange("h (w j) -> h w j", j=32)
        sch.op("vector", lambda e, a=h1_i, b=cw2_b, d=t3: e.tensor_tensor(d, a, b, ALU.mult))
        sch.op("vector", lambda e, d=h2_3, a=t3: e.tensor_tensor(d, d, a, ALU.add))
    v2 = sch.op("vector", lambda e: e.tensor_tensor(h2, h2, cp["cb2"], ALU.add), inc="v")
    sch.wait("scalar", "v", v2)
    s2 = sch.op("scalar", lambda e: e.activation(h2, h2, AF.Gelu), inc="s")
    sch.wait("vector", "s", s2)
    sch.op("vector", lambda e: e.tensor_tensor(tm3, h2, cp["cw3"], ALU.mult))
    sch.op("vector", lambda e: e.tensor_reduce(
        cwt, tm3.rearrange("h (w j) -> h w j", j=32), AX.X, ALU.add))
    vpad()
    sch.op("vector", lambda e: e.tensor_scalar(cwt, cwt, cp["cb3"], None, ALU.add))
    vpad()
    v3 = sch.op("vector", lambda e: e.reduce_max(csml[:, 0:1], cwt, AX.X, negate=True), inc="v")
    sch.wait("scalar", "v", v3)
    s3 = sch.op("scalar", lambda e: e.activation(cwt, cwt, AF.Exp, bias=csml[:, 0:1],
                                                 accum_out=csml[:, 1:2]), inc="s")
    sch.wait("vector", "s", s3)
    sch.op("vector", lambda e: e.reciprocal(csml[:, 2:3], csml[:, 1:2]))
    vpad()
    v4 = sch.op("vector", lambda e: e.tensor_scalar(cwt, cwt, csml[:, 2:3], None, ALU.mult), inc="v")
    sch.wait("sync", "v", v4)
    st1 = sch.op("sync", lambda e: e.dma_start(out=conn_d, in_=cwt), inc="st", dma=True)
    sch.wait("sync", "st", st1)
    ld1 = sch.op("sync", lambda e: e.dma_start(
        out=cfl[:], in_=conn_d.rearrange("h w -> (h w)").unsqueeze(0)), inc="ld", dma=True)
    sch.wait("tensor", "ld", ld1)
    t1 = sch.op("tensor", lambda e: e.matmul(ps_tr[0][:, 0:H * W7], ones1[:], cfl[:],
                                             start=True, stop=True), inc="t")
    t2 = sch.op("tensor", lambda e: e.matmul(ps_tr[1][:, 0:1], ones1[:], cp["wsv"],
                                             start=True, stop=True), inc="t")
    mm_done(ps_tr[0], t1)
    mm_done(ps_tr[1], t2)
    sch.wait("vector", "t", t2)
    connb = sall[:, 1024:1024 + H * W7]
    v5 = sch.op("vector", lambda e: e.tensor_copy(connb, ps_tr[0][:, 0:H * W7]), inc="v")
    ev_done(ps_tr[0], "v", v5)
    v6 = sch.op("vector", lambda e: e.tensor_copy(ws_b[:], ps_tr[1][:, 0:1]), inc="v")
    ev_done(ps_tr[1], "v", v6)
    sch.wait("vector", "v", init_v)
    cb_done = init_v
    for h in range(H):
        for w in range(W7):
            dmw = dmk[:, w * KW:(w + 1) * KW]
            dst = cbt[:, h * KW:(h + 1) * KW]
            scl = connb[:, h * W7 + w:h * W7 + w + 1]
            cb_done = sch.op("vector",
                             lambda e, dm=dmw, d=dst, s=scl: e.scalar_tensor_tensor(
                                 d, dm, s, d, ALU.mult, ALU.add),
                             inc="v")

    NBITS = 8
    NSEG = 2
    ev_xload = {}
    ev_xfree = {}
    ev_xnf = {}
    ev_kstrip = {}
    ev_vhead = {}
    ev_proj_read = {}
    ev_xn_tr = {}
    ev_xi_tr = {}
    ev_attn_bw = {}

    def ln_core(i, ch, gch):
        xs = xslot[gch % 2]
        sch.wait("vector", "ld", ev_xload[gch])
        sch.wait("scalar", "ld", ev_xload[gch])
        sch.op("vector", lambda e: e.reduce_sum(stA[:, 0:1], xs, AX.X))
        vpad()
        sch.wait("scalar", "t", ev_xi_tr.get(gch - 1, 0))
        sv = sch.op("scalar", lambda e: e.activation(xi_tm, xs, AF.Square,
                                                     accum_out=stB[:, 0:1]), inc="s")
        sch.wait("vector", "s", sv)
        sch.op("vector", lambda e: e.tensor_scalar(stB[:, 2:3], stA[:, 0:1], -1.0 / D, None, ALU.mult))
        vpad()
        sch.op("vector", lambda e: e.tensor_tensor(stA[:, 2:3], stB[:, 2:3], stB[:, 2:3], ALU.mult))
        vpad()
        sch.op("vector", lambda e: e.scalar_tensor_tensor(
            stC[:, 0:1], stB[:, 0:1], 1.0 / D, stA[:, 2:3], ALU.mult, ALU.subtract))
        vpad()
        vv = sch.op("vector", lambda e: e.tensor_scalar(
            stA[:, 6:7], stC[:, 0:1], 1.0, 1e-5, ALU.mult, ALU.add), inc="v")
        sch.wait("scalar", "v", vv)
        sch.op("scalar", lambda e: e.activation(stB[:, 6:7], stA[:, 6:7], AF.Ln))
        sv2 = sch.op("scalar", lambda e: e.activation(stC[:, 2:3], stB[:, 6:7], AF.Exp, scale=-0.5), inc="s")
        sch.wait("vector", "s", sv2)
        vv2 = sch.op("vector", lambda e: e.tensor_tensor(stA[:, 8:9], stB[:, 2:3], stC[:, 2:3], ALU.mult), inc="v")
        sch.wait("scalar", "v", vv2)
        sch.wait("scalar", "t", ev_xn_tr.get(gch - 1, 0))
        sxn = sch.op("scalar", lambda e: e.activation(xn_tm, xs, AF.Identity,
                                                      bias=stA[:, 8:9], scale=stC[:, 2:3]), inc="s")
        sch.wait("vector", "s", sxn)
        sch.op("vector", lambda e: e.reduce_max(amax[:, ch:ch + 1], xn_tm, AX.X,
                                                apply_absolute_value=True))
        vpad()
        sch.op("vector", lambda e: e.tensor_scalar(stA[:, 12:13], amax[:, ch:ch + 1], 1e-5, None, ALU.max))
        vpad()
        sch.op("vector", lambda e: e.reciprocal(stB[:, 12:13], stA[:, 12:13]))
        vpad()
        sch.op("vector", lambda e: e.tensor_scalar(sqs[:, gch:gch + 1], stB[:, 12:13], 127.0, None, ALU.mult))
        vv3 = sch.op("vector", lambda e: e.scalar_tensor_tensor(
            stB[:, 8:9], stA[:, 6:7], 1e-6, stC[:, 0:1], ALU.mult, ALU.add), inc="v")
        sch.wait("scalar", "v", vv3)
        sv3 = sch.op("scalar", lambda e: e.activation(stA[:, 10:11], stB[:, 8:9], AF.Ln), inc="s")
        sch.wait("vector", "s", sv3)
        vv4 = sch.op("vector", lambda e: e.tensor_tensor(stC[:, 4:5], stB[:, 6:7], stA[:, 10:11], ALU.subtract), inc="v")
        sch.wait("scalar", "v", vv4)
        sv4 = sch.op("scalar", lambda e: e.activation(stB[:, 10:11], stC[:, 4:5], AF.Exp, scale=0.5), inc="s")
        sch.wait("vector", "s", sv4)
        sch.op("vector", lambda e: e.tensor_tensor(stot[:, ch:ch + 1], stB[:, 10:11], sqs[:, gch:gch + 1], ALU.mult))
        vpad()
        vxs = sch.op("vector", lambda e: e.tensor_scalar(xs, xn_tm, stot[:, ch:ch + 1], None, ALU.mult), inc="v")
        return sxn, vxs

    def xn_transposes(i, ch, gch, sxn):
        for b in range(2):
            psb = ps_tr[b]
            mm_guard(psb)
            sch.wait("tensor", "s", sxn)
            for k in range(4):
                dc = b * 4 + k
                tv = sch.op("tensor", lambda e, dc=dc, k=k, psb=psb: e.transpose(
                    psb[:, k * 128:(k + 1) * 128],
                    xn_tm[:, dc * 128:(dc + 1) * 128], ident), inc="t")
            mm_done(psb, tv)
            ev_xn_tr[gch] = tv
            ev_guard("vector", psb)
            dco = b * 4
            hidst = xnf_hi.rearrange("p (c n) -> p c n", n=TA + 6)[:, dco:dco + 4, 3 + ch * 128:3 + ch * 128 + 128]
            lodst = xnf_lo.rearrange("p (c n) -> p c n", n=TA + 6)[:, dco:dco + 4, 3 + ch * 128:3 + ch * 128 + 128]
            ps3 = psb[:, 0:512].rearrange("p (c n) -> p c n", n=128)
            sch.op("vector", lambda e, d=hidst, s=ps3: e.tensor_copy(d, s))
            ve = sch.op("vector", lambda e, d=lodst, s=ps3, h=hidst: e.tensor_tensor(d, s, h, ALU.subtract), inc="v")
            ev_done(psb, "v", ve)
    def round_math(i, ch, gch):
        xs = xslot[gch % 2]
        sch.wait("vector", "t", ev_xi_tr.get(gch - 1, 0))
        xiv = None
        for qc in range(NSEG):
            seg = slice(qc * 512, (qc + 1) * 512)
            sch.op("vector", lambda e, s2=seg: e.tensor_scalar(u_b, xs[:, s2], -1.0, -128.5, ALU.mult, ALU.add))
            vpad()
            for b in range(NBITS - 1, -1, -1):
                tb = float(2 ** b)
                sch.op("vector", lambda e, tb=tb: e.tensor_scalar(
                    c_b, u_b, -tb, tb, ALU.is_le, ALU.mult))
                sch.op("vector", lambda e: e.tensor_tensor(u_b, u_b, c_b, ALU.add))
            xiv = sch.op("vector", lambda e, s2=seg: e.scalar_tensor_tensor(
                xi_tm[:, s2], u_b, 0.5, xs[:, s2], ALU.add, ALU.add), inc="v")
        ev_xfree[gch] = xiv
        return xiv

    def xi_tr_part(i, ch, gch, xiv):
        for b in range(2):
            psb = ps_tr[b]
            mm_guard(psb)
            sch.wait("tensor", "v", xiv)
            for k in range(4):
                dc = b * 4 + k
                tv = sch.op("tensor", lambda e, dc=dc, k=k, psb=psb: e.transpose(
                    psb[:, k * 128:(k + 1) * 128],
                    xi_tm[:, dc * 128:(dc + 1) * 128], ident), inc="t")
            mm_done(psb, tv)
            ev_xi_tr[gch] = tv
            ev_guard("vector", psb)
            dco = b * 4
            dst = xi_fm.rearrange("p (c n) -> p c n", n=128)[:, dco:dco + 4, :]
            ps3 = psb[:, 0:512].rearrange("p (c n) -> p c n", n=128)
            ve = sch.op("vector", lambda e, d=dst, s2=ps3: e.tensor_copy(d, s2), inc="v")
            ev_done(psb, "v", ve)
        sch.wait("sync", "v", ve)
        sch.op("sync", lambda e, g=gch: e.dma_start(
            out=xint_d[:, :, g * 128:(g + 1) * 128],
            in_=xi_fm.rearrange("p (c n) -> p c n", n=128)), inc="st", dma=True)

    def mm3(psum_ap, ps_handle, st_hi, st_lo, mv_hi, mv_lo, first, last, tp=None):
        sch.op("tensor", lambda e: e.matmul(psum_ap, st_hi, mv_hi, start=first, stop=False, tile_position=tp), inc="t")
        sch.op("tensor", lambda e: e.matmul(psum_ap, st_hi, mv_lo, start=False, stop=False, tile_position=tp), inc="t")
        return sch.op("tensor", lambda e: e.matmul(psum_ap, st_lo, mv_hi, start=False, stop=last, tile_position=tp), inc="t")

    SERIAL = os.environ.get("K_SERIAL") == "1"
    ev_sall_free = [0]
    ev_qproj_s = {}
    ev_kstrip_s = {}

    def emit_kproj(i):
        kb = kbuf[i % 2]
        NKC = TA + 3
        ve = 0
        for oc in range(NDC):
            psum = ps_pj[oc % 2]
            mm_guard(psum)
            for dc in range(NDC):
                tv = mm3(psum[:, 0:NKC], psum,
                         wn["wk_hi"][:, dc * D + oc * 128:dc * D + oc * 128 + 128],
                         wn["wk_lo"][:, dc * D + oc * 128:dc * D + oc * 128 + 128],
                         xnf_hi[:, dc * (TA + 6):dc * (TA + 6) + NKC],
                         xnf_lo[:, dc * (TA + 6):dc * (TA + 6) + NKC],
                         dc == 0, dc == NDC - 1)
            mm_done(psum, tv)
            ev_guard("scalar", psum)
            ve = sch.op("scalar", lambda e, o=oc, p=psum, kk=kb: e.copy(
                kk[:, o * (TA + 6):o * (TA + 6) + NKC], p[:, 0:NKC]), inc="s")
            ev_done(psum, "s", ve)
        if i >= 1:
            kprev = kbuf[(i - 1) % 2]
            src3 = kb.rearrange("p (c n) -> p c n", n=TA + 6)[:, :, 3:6]
            dst3 = kprev.rearrange("p (c n) -> p c n", n=TA + 6)[:, :, TA + 3:TA + 6]
            ev_kstrip_s[i - 1] = sch.op("scalar", lambda e, s2=src3, d=dst3: e.copy(d, s2), inc="s")
        return ve

    def emit_vhead(i):
        ve = 0
        for ob2 in range(2):
            psum = ps_pj[ob2]
            mm_guard(psum)
            for dc in range(NDC):
                tv = mm3(psum[:], psum,
                         xnf_hi[:, dc * (TA + 6):dc * (TA + 6) + 128],
                         xnf_lo[:, dc * (TA + 6):dc * (TA + 6) + 128],
                         wn["wv_hi"][:, dc * D + ob2 * 512:dc * D + ob2 * 512 + 512],
                         wn["wv_lo"][:, dc * D + ob2 * 512:dc * D + ob2 * 512 + 512],
                         dc == 0, dc == NDC - 1)
            mm_done(psum, tv)
            ev_guard("scalar", psum)
            ve = sch.op("scalar", lambda e, o=ob2, p=psum: e.copy(
                vhead[:, o * 512:(o + 1) * 512], p[:]), inc="s")
            ev_done(psum, "s", ve)
        return ("s", ve)

    def emit_scores(ti, ch):
        kprev = kbuf[ti % 2]
        for b0, nb in [(0, 3), (3, 3), (6, 2)]:
            mm_guard(ps_sc[0])
            mm_guard(ps_sc[1])
            for pi in range(b0, b0 + nb):
                sl = pi - b0
                qs0 = qbuf[0:64, pi * TA + ch * 128:pi * TA + ch * 128 + 128]
                qs1 = qbuf[64:128, pi * TA + ch * 128:pi * TA + ch * 128 + 128]
                ks0 = kprev[0:64, pi * (TA + 6) + ch * 128:pi * (TA + 6) + ch * 128 + KW]
                ks1 = kprev[64:128, pi * (TA + 6) + ch * 128:pi * (TA + 6) + ch * 128 + KW]
                sch.op("tensor", lambda e, q=qs0, k2=ks0, sl_=sl: e.matmul(
                    ps_sc[0][:, sl_ * KW:(sl_ + 1) * KW], q, k2, start=True, stop=True,
                    tile_position=(0, 0)), inc="t")
                tv = sch.op("tensor", lambda e, q=qs1, k2=ks1, sl_=sl: e.matmul(
                    ps_sc[1][:, sl_ * KW:(sl_ + 1) * KW], q, k2, start=True, stop=True,
                    tile_position=(64, 0)), inc="t")
            mm_done(ps_sc[0], tv)
            mm_done(ps_sc[1], tv)
            for par in range(2):
                psb = ps_sc[par]
                ev_guard("vector", psb)
                sch.wait("vector", "t", ev_sall_free[0])
                h0 = 2 * b0 + par
                dst = sall.rearrange("p (h n) -> p h n", n=KW)[:, h0:h0 + 2 * nb - 1:2, :]
                src = psb[:, 0:nb * KW].rearrange("p (s n) -> p s n", n=KW)
                m1b = m1t.unsqueeze(1).broadcast_to([128, nb, KW])
                ve = sch.op("vector", lambda e, d=dst, s2=src, m=m1b: e.scalar_tensor_tensor(
                    d, s2, 0.125, m, ALU.mult, ALU.add), inc="v")
                ev_done(psb, "v", ve)

    def emit_softmax(ti, ch):
        vm = sch.op("vector", lambda e: e.reduce_max(negm[:], sall, AX.X, negate=True), inc="v")
        sch.wait("scalar", "v", vm)
        se = sch.op("scalar", lambda e: e.activation(sall, sall, AF.Exp, bias=negm[:]), inc="s")
        sch.wait("vector", "s", se)
        sch.op("vector", lambda e: e.tensor_reduce(
            abd_a[:], sall.rearrange("p (h n) -> p h n", n=KW), AX.X, ALU.add))
        for h in range(H):
            ssl = sall[:, h * KW:(h + 1) * KW]
            csl = cbt[:, h * KW:(h + 1) * KW]
            sch.op("vector", lambda e, s2=ssl, c=csl, h=h: e.scalar_tensor_tensor(
                s2, s2, 1.0, c, ALU.mult, ALU.mult, accum_out=abd_b[:, h:h + 1]))
        vpad()
        sch.op("vector", lambda e: e.scalar_tensor_tensor(
            abd_d[:], abd_a[:], 1e-9, abd_b[:], ALU.mult, ALU.add))
        vpad()
        sch.op("vector", lambda e: e.reciprocal(abd_i[:], abd_d[:]))
        vpad()
        fw_evs = []
        for h in range(H):
            ssl = sall[:, h * KW:(h + 1) * KW]
            vf = sch.op("vector", lambda e, s2=ssl, h=h: e.tensor_scalar(
                s2, s2, abd_i[:, h:h + 1], None, ALU.mult), inc="v")
            fw_evs.append(vf)
        if _dbga:
            sch.wait("sync", "v", vf)
            dsv = sch.op("sync", lambda e, g=ti * CH_A + ch: e.dma_start(
                out=dbg_sall[g], in_=sall), inc="st", dma=True)
            sch.wait("tensor", "st", dsv)
        return fw_evs

    def emit_trout(ti, ch, fw_evs):
        gch = ti * CH_A + ch
        ev_qta_read = 0
        last_out = 0
        for tb in range(4):
            mm_guard(ps_tr[0])
            mm_guard(ps_tr[1])
            sch.wait("tensor", "v", fw_evs[tb * 4 + 3])
            for k in range(4):
                h = tb * 4 + k
                sch.op("tensor", lambda e, h=h, k=k: e.transpose(
                    ps_tr[0][:, k * 128:(k + 1) * 128],
                    sall[:, h * KW:h * KW + 128], ident), inc="t")
                tv = sch.op("tensor", lambda e, h=h, k=k: e.transpose(
                    ps_tr[1][0:6, k * 128:(k + 1) * 128],
                    sall[:, h * KW + 128:h * KW + KW], ident), inc="t")
            mm_done(ps_tr[0], tv)
            mm_done(ps_tr[1], tv)
            if tb == 3:
                ev_sall_free[0] = tv
            ev_guard("scalar", ps_tr[0])
            ev_guard("scalar", ps_tr[1])
            sch.wait("scalar", "t", ev_qta_read)
            sch.op("scalar", lambda e: e.copy(qta, ps_tr[0][:, 0:512]))
            ve = sch.op("scalar", lambda e: e.copy(qtb[0:6, :], ps_tr[1][0:6, 0:512]), inc="s")
            ev_done(ps_tr[0], "s", ve)
            ev_done(ps_tr[1], "s", ve)
            sch.wait("tensor", "s", ve)
            psb = ps_ou[tb // 2]
            if tb % 2 == 0:
                mm_guard(psb)
            for pp in range(2):
                p = tb * 2 + pp
                for hh in range(2):
                    h = 2 * p + hh
                    slot = h % 4
                    v1s = vbuf[:, ch * D + h * 64:ch * D + h * 64 + 64]
                    if ch < CH_A - 1:
                        v2s = vbuf[0:32, (ch + 1) * D + h * 64:(ch + 1) * D + h * 64 + 64]
                    else:
                        v2s = vhead[0:32, h * 64:h * 64 + 64]
                    q1 = qta[:, slot * 128:(slot + 1) * 128]
                    q2 = qtb[0:32, slot * 128 + 122:slot * 128 + 128]
                    po = psb[hh * 64:(hh + 1) * 64, (p % 4) * 128:(p % 4) * 128 + 128]
                    po2 = psb[hh * 64:(hh + 1) * 64, (p % 4) * 128 + 122:(p % 4) * 128 + 128]
                    sch.op("tensor", lambda e, v=v1s, q=q1, o=po, hh=hh: e.matmul(
                        o, v, q, start=True, stop=False, tile_position=(0, hh * 64)), inc="t")
                    tv = sch.op("tensor", lambda e, v=v2s, q=q2, o=po2, hh=hh: e.matmul(
                        o, v, q, start=False, stop=True, tile_position=(0, hh * 64)), inc="t")
            mm_done(psb, tv)
            ev_qta_read = tv
            last_out = tv
            if tb % 2 == 1:
                ev_guard("scalar", psb)
                sch.wait("scalar", "st", ev_attn_bw.get(gch, 0))
                k4 = tb // 2
                dst = attn_b.rearrange("p (c n) -> p c n", n=128)[:, 0:4, :]
                ve = sch.op("scalar",
                            lambda e, d=dst, s2=psb[:, 0:512].rearrange("p (c n) -> p c n", n=128):
                            e.copy(d, s2), inc="s")
                ev_done(psb, "s", ve)
                sch.wait("sync", "s", ve)
                sv = sch.op("sync", lambda e, g=gch, k4=k4: e.dma_start(
                    out=attn_d[:, 4 * k4:4 * k4 + 4, g * 128:(g + 1) * 128],
                    in_=attn_b.rearrange("p (c n) -> p c n", n=128)[:, 0:4, :]),
                    inc="st", dma=True)
                ev_attn_bw[gch] = sv
        return last_out

    def emit_qproj(i):
        ve = 0
        for oc in range(NDC):
            psum = ps_pj[oc % 2]
            mm_guard(psum)
            for dc in range(NDC):
                tv = mm3(psum[:, 0:TA], psum,
                         wn["wq_hi"][:, dc * D + oc * 128:dc * D + oc * 128 + 128],
                         wn["wq_lo"][:, dc * D + oc * 128:dc * D + oc * 128 + 128],
                         xnf_hi[:, dc * (TA + 6) + 3:dc * (TA + 6) + 3 + TA],
                         xnf_lo[:, dc * (TA + 6) + 3:dc * (TA + 6) + 3 + TA],
                         dc == 0, dc == NDC - 1)
            mm_done(psum, tv)
            ev_guard("scalar", psum)
            ve = sch.op("scalar", lambda e, o=oc, p=psum: e.copy(
                qbuf[:, o * TA:(o + 1) * TA], p[:, 0:TA]), inc="s")
            ev_done(psum, "s", ve)
        ev_qproj_s[i] = ve

    def emit_vproj_rest(i, ev_vbuf_free):
        vh_s, vh_v = ev_vhead[i]
        sch.wait("vector", vh_s, vh_v)
        sch.wait("vector", "t", ev_vbuf_free)
        ve = sch.op("vector", lambda e: e.tensor_copy(vbuf[:, 0:D], vhead[:, 0:D]), inc="v")
        for ob2 in range(2):
            psum = ps_pj[ob2]
            mm_guard(psum)
            for dc in range(NDC):
                tv = mm3(psum[:], psum,
                         xnf_hi[:, dc * (TA + 6) + 128:dc * (TA + 6) + 256],
                         xnf_lo[:, dc * (TA + 6) + 128:dc * (TA + 6) + 256],
                         wn["wv_hi"][:, dc * D + ob2 * 512:dc * D + ob2 * 512 + 512],
                         wn["wv_lo"][:, dc * D + ob2 * 512:dc * D + ob2 * 512 + 512],
                         dc == 0, dc == NDC - 1)
            mm_done(psum, tv)
            ev_guard("vector", psum)
            ve = sch.op("vector", lambda e, o=ob2, p=psum: e.tensor_copy(
                vbuf[:, D + o * 512:D + o * 512 + 512], p[:, 0:512]), inc="v")
            ev_done(psum, "v", ve)
        return ve

    for i in range(NT_A + 1):
        if SERIAL:
            sch.barrier()
        ti = i - 1
        if i < NT_A:
            for ch in range(CH_A):
                gch = i * CH_A + ch
                sch.wait("sync", "v", ev_xfree.get(gch - 2, 0))
                ev_xload[gch] = sch.op("sync", lambda e, g=gch, sl=xslot[gch % 2]: e.dma_start(
                    out=sl, in_=xio.rearrange("(c p) d -> p c d", p=128)[:, g, :]),
                    inc="ld", dma=True)

        if i < NT_A:
            sch.wait("vector", "t", ev_proj_read.get(i - 1, 0))
            x3h = xnf_hi.rearrange("p (c n) -> p c n", n=TA + 6)[:, :, 0:3]
            x3l = xnf_lo.rearrange("p (c n) -> p c n", n=TA + 6)[:, :, 0:3]
            s3h = strip_hi.rearrange("p (c n) -> p c n", n=3)
            s3l = strip_lo.rearrange("p (c n) -> p c n", n=3)
            sch.op("vector", lambda e: e.tensor_copy(x3h, s3h))
            sch.op("vector", lambda e: e.tensor_copy(x3l, s3l))
            for ch in range(CH_A):
                gch = i * CH_A + ch
                sxn, _ = ln_core(i, ch, gch)
                xn_transposes(i, ch, gch, sxn)
            ev_xnf[i] = sch.cnt["v"]
            sph = xnf_hi.rearrange("p (c n) -> p c n", n=TA + 6)[:, :, TA:TA + 3]
            spl = xnf_lo.rearrange("p (c n) -> p c n", n=TA + 6)[:, :, TA:TA + 3]
            sch.op("vector", lambda e: e.tensor_copy(s3h, sph))
            sch.op("vector", lambda e: e.tensor_copy(s3l, spl))

        if i == 0:
            sch.wait("tensor", "v", ev_xnf[0])
            emit_kproj(0)
            ev_vhead[0] = emit_vhead(0)
            emit_qproj(0)
            emit_vproj_rest(0, 0)
            xv = round_math(0, 0, 0)
            xi_tr_part(0, 0, 0, xv)
            xv = round_math(0, 1, 1)
            xi_tr_part(0, 1, 1, xv)
            ev_proj_read[0] = sch.cnt["t"]
            continue

        if SERIAL:
            sch.barrier()
        sch.wait("tensor", "v", cb_done)
        sch.wait("tensor", "s", ev_qproj_s.get(ti, 0))
        if _dbga:
            sch.wait("tensor", "s", ev_kstrip_s.get(ti, 0))
            kprev = kbuf[ti % 2]
            sch.op("sync", lambda e, t=ti: e.dma_start(out=dbg_q[t], in_=qbuf), inc="st", dma=True)
            sch.op("sync", lambda e, t=ti, kp=kprev: e.dma_start(out=dbg_k[t], in_=kp), inc="st", dma=True)
            sch.op("sync", lambda e, t=ti: e.dma_start(out=dbg_v[t], in_=vbuf), inc="st", dma=True)
            dbv = sch.op("sync", lambda e, t=ti: e.dma_start(out=dbg_vh[t], in_=vhead), inc="st", dma=True)
            sch.wait("tensor", "st", dbv)

        emit_scores(ti, 0)
        fw0 = emit_softmax(ti, 0)
        if i < NT_A:
            sch.wait("tensor", "v", ev_xnf[i])
            emit_kproj(i)
            ev_vhead[i] = emit_vhead(i)
        else:
            kprev = kbuf[(i - 1) % 2]
            dst3 = kprev.rearrange("p (c n) -> p c n", n=TA + 6)[:, :, TA + 3:TA + 6]
            ev_kstrip_s[i - 1] = sch.op("scalar", lambda e, d=dst3: e.memzero(d), inc="s")
            vz = sch.op("vector", lambda e: e.memset(vhead, 0.0), inc="v")
            sch.wait("tensor", "v", vz)
            for ob2 in range(2):
                psum = ps_pj[ob2]
                mm_guard(psum)
                for dc in range(NDC):
                    tv = mm3(psum[0:3, :], psum,
                             strip_hi[:, dc * 3:dc * 3 + 3],
                             strip_lo[:, dc * 3:dc * 3 + 3],
                             wn["wv_hi"][:, dc * D + ob2 * 512:dc * D + ob2 * 512 + 512],
                             wn["wv_lo"][:, dc * D + ob2 * 512:dc * D + ob2 * 512 + 512],
                             dc == 0, dc == NDC - 1)
                mm_done(psum, tv)
                ev_guard("vector", psum)
                vz = sch.op("vector", lambda e, o=ob2, p=psum: e.tensor_copy(
                    vhead[0:3, o * 512:(o + 1) * 512], p[0:3, :]), inc="v")
                ev_done(psum, "v", vz)
            ev_vhead[i] = ("v", vz)
        vh_sem, vh_val = ev_vhead[i]
        sch.wait("tensor", vh_sem, vh_val)
        out0 = emit_trout(ti, 0, fw0)
        if i < NT_A:
            xiv0 = round_math(i, 0, i * CH_A)
            xi_tr_part(i, 0, i * CH_A, xiv0)
        sch.wait("tensor", "s", ev_kstrip_s.get(ti, 0))
        emit_scores(ti, 1)
        fw1 = emit_softmax(ti, 1)
        if i < NT_A:
            emit_qproj(i)
            xiv1 = round_math(i, 1, i * CH_A + 1)
            xi_tr_part(i, 1, i * CH_A + 1, xiv1)
        out1 = emit_trout(ti, 1, fw1)
        if i < NT_A:
            emit_vproj_rest(i, out0)
        ev_proj_read[i] = sch.cnt["t"]

    finA = {s: sch.cnt[s] for s in ("t", "v", "s", "g", "st")}

    for eng in ("sync", "tensor", "vector", "scalar", "gpsimd"):
        for s in ("t", "v", "s", "g"):
            if (eng, s) not in (("tensor", "t"), ("vector", "v"), ("scalar", "s"), ("gpsimd", "g")):
                sch.wait(eng, s, finA[s])
    sch.wait("sync", "st", finA["st"])

    stageB = xb2[:, 0:2 * D]
    last_split = 0
    for piece in range(4):
        dc0 = piece * 2
        sch.wait("sync", "v", last_split)
        lv = sch.op("sync",
                    lambda e, dc0=dc0: e.dma_start(
                        out=stageB,
                        in_=woT.rearrange("(c p) o -> p c o", p=128)[:, dc0:dc0 + 2, :]),
                    inc="ld", dma=True)
        sch.wait("vector", "ld", lv)
        for j in range(2):
            dc = dc0 + j
            src = stageB[:, j * D:(j + 1) * D]
            hi = wn["wq_hi"][:, dc * D:(dc + 1) * D]
            lo = wn["wq_lo"][:, dc * D:(dc + 1) * D]
            sch.op("vector", lambda e, s=src, h=hi: e.tensor_copy(h, s))
            last_split = sch.op("vector",
                                lambda e, s=src, h=hi, l=lo: e.tensor_tensor(l, s, h, ALU.subtract),
                                inc="v")
    lb1 = sch.op("sync", lambda e: e.dma_start(out=bo_t, in_=bo_rep), inc="ld", dma=True)
    lb2 = sch.op("sync", lambda e: e.dma_start(out=bf_t, in_=bf_rep), inc="ld", dma=True)
    sch.wait("vector", "ld", lb2)
    sch.op("vector", lambda e: e.reciprocal(rq_b[:], sqs[:]))
    vpad()
    gsv = sch.op("vector", lambda e: e.tensor_scalar(gs_b[:], rq_b[:], ws_b[:, 0:1], None, ALU.mult), inc="v")
    wo_ready = last_split
    sch.wait("sync", "v", wo_ready)

    ev_bld = {}
    ev_bfree = {}
    ev_outdma = {}
    ev_mm_read = 0

    def emit_b_loads(j):
        t0 = j * TB
        par = j % 2
        sch.wait("sync", "st", ev_outdma.get(j - 1, 0))
        sch.wait("sync", "v", ev_bfree.get(j - 1, 0))
        l1 = sch.op("sync", lambda e, t0=t0: e.dma_start(
            out=ab_buf.rearrange("p (c n) -> p c n", n=TB),
            in_=attn_d[:, :, t0:t0 + TB]), inc="ld", dma=True)
        l2 = sch.op("sync", lambda e, t0=t0, p=par: e.dma_start(
            out=xib[p].rearrange("p (c n) -> p c n", n=TB),
            in_=xint_d[:, :, t0:t0 + TB]), inc="ld", dma=True)
        l3 = sch.op("sync", lambda e, t0=t0: e.dma_start(
            out=xb2.rearrange("p (c n) -> p c n", n=D),
            in_=xio.rearrange("(c p) d -> p c d", p=128)[:, t0 // 128:t0 // 128 + CH_B, :]),
            inc="ld", dma=True)
        ev_bld[j] = (l1, l2, l3)

    emit_b_loads(0)
    for j in range(1, NT_B + 1):
        jj = j - 1
        par = jj % 2
        t0 = jj * TB
        l1, l2, l3 = ev_bld[jj]
        sch.wait("vector", "ld", l1)
        sch.wait("vector", "t", ev_mm_read)
        for dc in range(NDC):
            src = ab_buf[:, dc * TB:(dc + 1) * TB]
            hi = ah_b[:, dc * TB:(dc + 1) * TB]
            lo = al_b[:, dc * TB:(dc + 1) * TB]
            sch.op("vector", lambda e, s=src, h=hi: e.tensor_copy(h, s))
            vsp = sch.op("vector", lambda e, s=src, h=hi, l=lo: e.tensor_tensor(l, s, h, ALU.subtract), inc="v")
        s_tm = ab_buf
        sch.wait("tensor", "v", vsp)
        for tc in range(CH_B):
            for ob2 in range(2):
                psum = ps_pj[(tc * 2 + ob2) % 2]
                mm_guard(psum)
                for dc in range(NDC):
                    tv = mm3(psum[:], psum,
                             ah_b[:, dc * TB + tc * 128:dc * TB + tc * 128 + 128],
                             al_b[:, dc * TB + tc * 128:dc * TB + tc * 128 + 128],
                             wn["wq_hi"][:, dc * D + ob2 * 512:dc * D + ob2 * 512 + 512],
                             wn["wq_lo"][:, dc * D + ob2 * 512:dc * D + ob2 * 512 + 512],
                             dc == 0, dc == NDC - 1)
                mm_done(psum, tv)
                ev_guard("vector", psum)
                ve = sch.op("vector", lambda e, t=tc, o=ob2, p=psum: e.scalar_tensor_tensor(
                    s_tm[:, t * D + o * 512:t * D + o * 512 + 512], p[:], 1.0,
                    bo_t[:, o * 512:(o + 1) * 512], ALU.mult, ALU.add), inc="v")
                ev_done(psum, "v", ve)
        sch.wait("tensor", "ld", l2)
        sch.wait("tensor", "v", gsv)
        for tc in range(CH_B):
            gch = jj * CH_B + tc
            for ob2 in range(2):
                psum = ps_ou[ob2]
                mm_guard(psum)
                for dc in range(NDC):
                    tv = sch.op("tensor", lambda e, dc=dc, p=psum, t=tc, o=ob2, pr=par: e.matmul(
                        p[:], xib[pr][:, dc * TB + t * 128:dc * TB + t * 128 + 128],
                        wi8_sb[:, dc * D + o * 512:dc * D + o * 512 + 512],
                        start=(dc == 0), stop=(dc == NDC - 1)), inc="t")
                mm_done(psum, tv)
                ev_guard("vector", psum)
                vg = sch.op("vector", lambda e, t=tc, o=ob2, p=psum, g=gch: e.scalar_tensor_tensor(
                    g_tm[:, t * D + o * 512:t * D + o * 512 + 512], p[:], gs_b[:, g:g + 1],
                    bf_t[:, o * 512:(o + 1) * 512], ALU.mult, ALU.add), inc="v")
                ev_done(psum, "v", vg)
            sch.wait("scalar", "v", vg)
            sg = sch.op("scalar", lambda e, t=tc: e.activation(
                g_tm[:, t * D:(t + 1) * D], g_tm[:, t * D:(t + 1) * D], AF.Silu), inc="s")
            sch.wait("vector", "s", sg)
            sch.wait("vector", "ld", l3)
            sch.op("vector", lambda e, t=tc: e.tensor_tensor(
                g_tm[:, t * D:(t + 1) * D], g_tm[:, t * D:(t + 1) * D],
                s_tm[:, t * D:(t + 1) * D], ALU.mult))
            sch.op("vector", lambda e, t=tc: e.tensor_tensor(
                xb2[:, t * D:(t + 1) * D], xb2[:, t * D:(t + 1) * D],
                g_tm[:, t * D:(t + 1) * D], ALU.add))
        ev_mm_read = sch.cnt["t"]
        vfree = sch.op("vector", lambda e: e.engine_nop(), inc="v")
        ev_bfree[jj] = vfree
        sch.wait("sync", "v", vfree)
        ev_outdma[jj] = sch.op("sync", lambda e, t0=t0: e.dma_start(
            out=out_d.rearrange("(c p) d -> p c d", p=128)[:, t0 // 128:t0 // 128 + CH_B, :],
            in_=xb2.rearrange("p (c n) -> p c n", n=D)), inc="st", dma=True)
        if j < NT_B:
            emit_b_loads(j)

    sch.wait("sync", "st", sch.cnt["st"])

    with nc.Block() as block:
        @block.sync
        def _(sync):
            for fn in sch.prog["sync"]:
                fn(sync)

        @block.tensor
        def _(tensor):
            for fn in sch.prog["tensor"]:
                fn(tensor)

        @block.vector
        def _(vector):
            for fn in sch.prog["vector"]:
                fn(vector)

        @block.scalar
        def _(scalar):
            for fn in sch.prog["scalar"]:
                fn(scalar)

        @block.gpsimd
        def _(gpsimd):
            for fn in sch.prog["gpsimd"]:
                fn(gpsimd)

    return nc


def _host_consts():
    f32 = np.float32
    ident = np.eye(128, dtype=f32)
    ones1 = np.ones((1, 128), dtype=f32)
    t_idx = np.arange(128)[:, None]
    k_idx = np.arange(KW)[None, :]
    band = (k_idx - t_idx >= 0) & (k_idx - t_idx <= 6)
    m1 = np.where(band, 0.0, -np.inf).astype(f32)
    dmask = np.zeros((128, W7 * KW), dtype=f32)
    for w in range(W7):
        dmask[:, w * KW:(w + 1) * KW] = (k_idx - t_idx == w).astype(f32)
    return ident, ones1, m1, dmask


def kernel(L0_attn_x, L0_attn_wq, L0_attn_wk, L0_attn_wv, L0_attn_wo,
           L0_attn_bo, L0_attn_cw1, L0_attn_cb1, L0_attn_cw2, L0_attn_cb2,
           L0_attn_cw3, L0_attn_cb3, L0_ffn_w_int8, L0_ffn_weight_scale,
           L0_ffn_bias):
    x = np.ascontiguousarray(np.asarray(L0_attn_x, dtype=np.float32))
    B, S, d = x.shape
    assert d == D
    f32 = np.float32

    wqT = np.ascontiguousarray(np.asarray(L0_attn_wq, f32).T)
    wkT = np.ascontiguousarray(np.asarray(L0_attn_wk, f32).T)
    wvT = np.ascontiguousarray(np.asarray(L0_attn_wv, f32).T)
    woT = np.ascontiguousarray(np.asarray(L0_attn_wo, f32).T)
    wi8T = np.ascontiguousarray(np.asarray(L0_ffn_w_int8, np.int32).T)
    bo_rep = np.ascontiguousarray(np.broadcast_to(np.asarray(L0_attn_bo, f32), (128, D)))
    bf_rep = np.ascontiguousarray(np.broadcast_to(np.asarray(L0_ffn_bias, f32), (128, D)))
    ws = np.asarray(L0_ffn_weight_scale, f32).reshape(1, 1)

    pos = np.linspace(0.0, 1.0, W7, dtype=f32)
    pos_t = np.ascontiguousarray(np.broadcast_to(pos[None, :, None], (H, W7, 32)).reshape(H, 224))
    cw1 = np.asarray(L0_attn_cw1, f32)
    cb1 = np.asarray(L0_attn_cb1, f32)
    cw2 = np.asarray(L0_attn_cw2, f32)
    cb2 = np.asarray(L0_attn_cb2, f32)
    cw3 = np.asarray(L0_attn_cw3, f32)
    cb3 = np.asarray(L0_attn_cb3, f32)
    cw1r = np.ascontiguousarray(np.broadcast_to(cw1[:, None, :, 0], (H, W7, 32)).reshape(H, 224))
    cb1r = np.ascontiguousarray(np.broadcast_to(cb1[:, None, :], (H, W7, 32)).reshape(H, 224))
    cw2r = np.ascontiguousarray(cw2.reshape(H, 1024))
    cb2r = np.ascontiguousarray(np.broadcast_to(cb2[:, None, :], (H, W7, 32)).reshape(H, 224))
    cw3r = np.ascontiguousarray(np.broadcast_to(cw3, (H, W7, 32)).reshape(H, 224))
    cb3c = np.ascontiguousarray(cb3)

    ident, ones1, m1, dmask = _host_consts()

    nc = build_program(S)
    shared = dict(wqT=wqT, wkT=wkT, wvT=wvT, woT=woT, wi8T=wi8T,
                  bo_rep=bo_rep, bf_rep=bf_rep, ws=ws,
                  pos_t=pos_t, cw1r=cw1r, cb1r=cb1r, cw2r=cw2r, cb2r=cb2r,
                  cw3r=cw3r, cb3c=cb3c, ident=ident, ones1=ones1,
                  m1=m1, dmask=dmask)
    in_maps = [dict(shared, x=np.ascontiguousarray(x[b])) for b in range(B)]
    res = run_bass_kernel_spmd(nc, in_maps, core_ids=list(range(B)))
    if os.environ.get("K_DEBUG_SCRATCH") == "1":
        keys = ["attn_scr", "xint_scr", "conn_scr"]
        if os.environ.get("K_DEBUG_ATTN") == "1":
            keys += ["dbg_q", "dbg_k", "dbg_v", "dbg_vh", "dbg_sall", "dbg_k2"]
        for b in range(B):
            for k in keys:
                np.save(f"/tmp/k_dbg_{k}_{b}.npy", res.results[b][k])
    out = np.stack([r["out"] for r in res.results], axis=0)
    return out.astype(np.float32)
